# version 08
# baseline (speedup 1.0000x reference)
import numpy as np
import ml_dtypes
import concourse.bass as bass
import concourse.mybir as mybir
from concourse.bass_utils import run_bass_kernel_spmd
from contextlib import ExitStack

F32 = mybir.dt.float32
BF16 = mybir.dt.bfloat16
AF = mybir.ActivationFunctionType
ALU = mybir.AluOpType
AX = mybir.AxisListType

NCORES = 8
B, T, E, H, D = 16, 2048, 1024, 16, 64
SEQ_PER_CORE = B // NCORES
NTOK = SEQ_PER_CORE * T
FH = 4096
NE = 8
OC = E // 128
TILE = 512
NTILE = NTOK // TILE
TPS = T // TILE
KC = T // 128
QT = T // TILE
HC_HALF = 16
BF = ml_dtypes.bfloat16


class Sem:
    def __init__(self, name):
        self.name = name
        self.n = 0
        self.h = None

    def inc(self, k=1):
        self.n += k
        return self.n


class Core:
    ENGS = ("tensor", "vector", "scalar", "sync", "gpsimd")

    def __init__(self):
        self.items = {e: [] for e in self.ENGS}
        self.waited = {e: {} for e in self.ENGS}
        self.selfsem = {}

    def wait(self, eng, sem, val=None):
        v = sem.n if val is None else val
        if v <= 0:
            return
        w = self.waited[eng]
        if w.get(sem, -1) >= v:
            return
        w[sem] = v
        self.items[eng].append(lambda Eg, s=sem, vv=v: Eg.wait_ge(s.h, vv))

    def op(self, eng, build, inc=None, waits=(), dma=False):
        ss = self.selfsem.get(eng)
        if ss is not None and not dma:
            self.wait(eng, ss, ss.n)
        for wspec in waits:
            if isinstance(wspec, Sem):
                self.wait(eng, wspec)
            else:
                self.wait(eng, wspec[0], wspec[1])
        if inc is None:
            self.items[eng].append(build)
            return None
        sem, k = inc
        target = sem.inc(k)
        self.items[eng].append(
            lambda Eg, b=build, s2=sem, kk=k: b(Eg).then_inc(s2.h, kk))
        return target

    def raw(self, eng, f):
        self.items[eng].append(f)

    def barrier(self, sems):
        for eng in self.ENGS:
            for s in sems:
                self.wait(eng, s)

    def flush(self, block):
        items = {e: list(v) for e, v in self.items.items()}

        def replay(lst):
            def f(Eg):
                for it in lst:
                    it(Eg)
            return f
        for name, meth in (("tensor", block.tensor), ("vector", block.vector),
                           ("scalar", block.scalar), ("sync", block.sync),
                           ("gpsimd", block.gpsimd)):
            if items[name]:
                meth(replay(items[name]))
        for e in self.ENGS:
            self.items[e] = []


def host_prep(inputs):
    x = np.ascontiguousarray(inputs["L0_attn_x"], dtype=np.float32)
    per_core = []
    for c in range(NCORES):
        xc = np.ascontiguousarray(
            x[c * SEQ_PER_CORE:(c + 1) * SEQ_PER_CORE].reshape(NTOK, E))
        per_core.append({"x": xc})

    sh = {}
    sh["WqT"] = np.ascontiguousarray(inputs["L0_attn_Wq"].T.astype(np.float32))
    sh["WkT"] = np.ascontiguousarray(inputs["L0_attn_Wk"].T.astype(np.float32))
    sh["WvT"] = np.ascontiguousarray(inputs["L0_attn_Wv"].T.astype(np.float32))
    sh["WgT"] = np.ascontiguousarray(inputs["L0_attn_Wg"].T.astype(np.float32))
    sh["bq2"] = np.ascontiguousarray(
        inputs["L0_attn_bq"].astype(np.float32).reshape(OC, 128).T)
    sh["bk2"] = np.ascontiguousarray(
        inputs["L0_attn_bk"].astype(np.float32).reshape(OC, 128).T)
    sh["bvb"] = np.ascontiguousarray(
        np.broadcast_to(inputs["L0_attn_bv"].astype(np.float32), (128, E)))
    sh["bgb"] = np.ascontiguousarray(
        np.broadcast_to(inputs["L0_attn_bg"].astype(np.float32), (128, E)))
    sh["gammab"] = np.ascontiguousarray(
        np.broadcast_to(inputs["L0_attn_gamma"].astype(np.float32), (128, H)))
    sh["Wgf"] = np.ascontiguousarray(inputs["L0_ffn_Wg"].astype(np.float32))
    sh["bgfb"] = np.ascontiguousarray(
        np.broadcast_to(inputs["L0_ffn_bg"].astype(np.float32), (128, NE)))
    sh["W1"] = np.ascontiguousarray(inputs["L0_ffn_W1"].astype(np.float32))
    sh["b1r"] = np.ascontiguousarray(
        inputs["L0_ffn_b1"].astype(np.float32).reshape(NE, FH // 128, 128)
        .transpose(0, 2, 1))
    sh["W2"] = np.ascontiguousarray(inputs["L0_ffn_W2"].astype(np.float32))
    sh["b2"] = np.ascontiguousarray(inputs["L0_ffn_b2"].astype(np.float32))

    invf = (10000.0 ** (-np.arange(0, D, 2, dtype=np.float64) / D))
    p = np.arange(128)
    fidx = (p % D) % (D // 2)
    t = np.arange(T, dtype=np.float64)
    ang = t[None, :] * invf[fidx][:, None]
    sh["cosq"] = np.ascontiguousarray((np.cos(ang) / 8.0).astype(np.float32))
    sh["sinq"] = np.ascontiguousarray((np.sin(ang) / 8.0).astype(np.float32))
    sh["cosk"] = np.ascontiguousarray(np.cos(ang).astype(np.float32))
    sh["sink"] = np.ascontiguousarray(np.sin(ang).astype(np.float32))
    pm = np.zeros((128, 128), dtype=np.float32)
    for j in range(64):
        pm[2 * j + 1, 2 * j] = -1.0
        pm[2 * j, 2 * j + 1] = 1.0
    sh["PM"] = pm.astype(BF)
    sh["ident"] = np.eye(128, dtype=BF)
    c = np.arange(2 * T, dtype=np.float32)
    sh["DT"] = np.ascontiguousarray(
        (c[None, :] - float(T - 1)) - p[:, None].astype(np.float32))
    j = np.arange(128, dtype=np.float32)
    sh["DTD"] = np.ascontiguousarray(j[None, :] - j[:, None])
    sh["ROWJT8"] = np.ascontiguousarray(np.tile(j, (128, 8)))
    sh["COLJR"] = np.ascontiguousarray((128.0 - j)[:, None])
    return per_core, sh


def build_nc(debug=()):
    nc = bass.Bass("TRN2", target_bir_lowering=False)

    def din(name, shape, dt=F32):
        return nc.dram_tensor(name, shape, dt, kind="ExternalInput").ap()

    x = din("x", [NTOK, E])
    WqT, WkT = din("WqT", [E, E]), din("WkT", [E, E])
    WvT, WgT = din("WvT", [E, E]), din("WgT", [E, E])
    bq2, bk2 = din("bq2", [128, OC]), din("bk2", [128, OC])
    bvb, bgb = din("bvb", [128, E]), din("bgb", [128, E])
    gammab = din("gammab", [128, H])
    Wgf = din("Wgf", [E, NE])
    bgfb = din("bgfb", [128, NE])
    W1 = din("W1", [NE, E, FH])
    b1r = din("b1r", [NE, 128, FH // 128])
    W2 = din("W2", [NE, FH, E])
    b2 = din("b2", [NE, E])
    cosq, sinq = din("cosq", [128, T]), din("sinq", [128, T])
    cosk, sink = din("cosk", [128, T]), din("sink", [128, T])
    PM = din("PM", [128, 128], BF16)
    ident = din("ident", [128, 128], BF16)
    DTt = din("DT", [128, 2 * T])
    DTD = din("DTD", [128, 128])
    ROWJT8 = din("ROWJT8", [128, 8 * 128])
    COLJR = din("COLJR", [128, 1])

    def scratch(name, shape, dt):
        kind = "ExternalOutput" if name in debug else "Internal"
        return nc.dram_tensor(name, shape, dt, kind=kind).ap()

    Qs = scratch("Qs", [SEQ_PER_CORE, OC, 128, T], BF16)
    Ks = scratch("Ks", [SEQ_PER_CORE, OC, 128, T], BF16)
    Vs = scratch("Vs", [SEQ_PER_CORE, KC, 128, E], BF16)
    Gs = scratch("Gs", [SEQ_PER_CORE, KC, 128, E], F32)
    Ofm = scratch("Ofm", [SEQ_PER_CORE, OC, 128, T], BF16)
    X1f = scratch("X1f", [NTOK // 128, 128, E], F32)
    X1bf = scratch("X1bf", [NTILE, OC, 128, TILE], BF16)
    Ys = scratch("Ys", [NTOK // 128, 128, E], F32)
    y = nc.dram_tensor("y", [NTOK, E], F32, kind="ExternalOutput").ap()

    co = Core()
    sems = {}

    def S_(name):
        s = Sem(name)
        sems[name] = s
        return s

    sT, sV, sS = S_("sT"), S_("sV"), S_("sS")
    co.selfsem = {"vector": sV, "scalar": sS}
    d_const = S_("d_const")
    d_constg = S_("d_constg")
    d_wp = S_("d_wp")
    d_wpg = S_("d_wpg")
    d_x = [S_("d_x0"), S_("d_x1")]
    d_spq = [S_("d_spq0"), S_("d_spq1"), S_("d_spq2")]
    d_spv = [S_("d_spv0"), S_("d_spv1")]
    d_spg = [S_("d_spg0"), S_("d_spg1")]
    d_qkv = S_("d_qkv")
    d_dt = S_("d_dt")
    d_g = [S_("d_g0"), S_("d_g1")]
    d_xf = [S_("d_xf0"), S_("d_xf1")]
    d_x1f = [S_("d_x1f0"), S_("d_x1f1")]
    d_x1b = [S_("d_x1b0"), S_("d_x1b1")]
    d_ysi = S_("d_ysi")
    d_w1 = [S_("d_w10"), S_("d_w11")]
    d_w2 = [S_("d_w20"), S_("d_w21")]
    d_x1t = [S_("d_x1t0"), S_("d_x1t1")]
    d_yacc = [S_("d_yacc0"), S_("d_yacc1"), S_("d_yacc2")]
    d_ye = [S_("d_ye0"), S_("d_ye1")]
    d_out = [S_("d_out0"), S_("d_out1")]

    with ExitStack() as ctx:
        blk = ctx.enter_context(nc.Block())
        for s in sems.values():
            s.h = ctx.enter_context(nc.semaphore(s.name))

        b1_sb = ctx.enter_context(nc.sbuf_tensor("b1_sb", [128, NE * (FH // 128)], F32))
        b2bf = ctx.enter_context(nc.sbuf_tensor("b2bf", [NE, E], BF16))
        bgf_sb = ctx.enter_context(nc.sbuf_tensor("bgf_sb", [128, NE], F32))
        Wgf_sb = ctx.enter_context(nc.sbuf_tensor("Wgf_sb", [128, OC * NE], BF16))
        gneg = ctx.enter_context(nc.sbuf_tensor("gneg", [128, H], F32))
        gtmp = ctx.enter_context(nc.sbuf_tensor("gtmp", [128, H], F32))
        maskW = ctx.enter_context(nc.sbuf_tensor("maskW", [128, NTILE * 4 * NE], F32))
        id_sb = ctx.enter_context(nc.sbuf_tensor("id_sb", [128, 128], BF16))
        pm_sb = ctx.enter_context(nc.sbuf_tensor("pm_sb", [128, 128], BF16))
        gcol = ctx.enter_context(nc.sbuf_tensor("gcol", [128, NE], F32))
        mdiag = ctx.enter_context(nc.sbuf_tensor("mdiag", [128, H * 128], F32))
        decfm = ctx.enter_context(nc.sbuf_tensor("decfm", [128, E], BF16))
        decBtm = ctx.enter_context(nc.sbuf_tensor("decBtm", [128, E], BF16))
        dc128 = ctx.enter_context(nc.sbuf_tensor("dc128", [128, E], F32))

        co.op("sync", lambda E_: E_.dma_start(
            out=b1_sb[:].rearrange("p (e h) -> p e h", e=NE),
            in_=b1r.rearrange("e p h -> p e h")), inc=(d_const, 16), dma=True)
        co.op("sync", lambda E_: E_.dma_start(out=bgf_sb[:], in_=bgfb),
              inc=(d_const, 16), dma=True)
        co.op("sync", lambda E_: E_.dma_start(out=id_sb[:], in_=ident),
              inc=(d_const, 16), dma=True)
        co.op("sync", lambda E_: E_.dma_start(out=pm_sb[:], in_=PM),
              inc=(d_const, 16), dma=True)
        co.op("sync", lambda E_: E_.dma_start(out=gtmp[:], in_=gammab),
              inc=(d_const, 16), dma=True)
        co.op("gpsimd", lambda E_: E_.dma_start(out=b2bf[:], in_=b2),
              inc=(d_constg, 16), dma=True)
        co.op("gpsimd", lambda E_: E_.dma_start(
            out=Wgf_sb[:].rearrange("p (o e) -> p o e", o=OC),
            in_=Wgf.rearrange("(o p) e -> p o e", p=128)),
            inc=(d_constg, 16), dma=True)
        co.op("scalar", lambda E_: E_.activation(gneg[:], gtmp[:], AF.Sigmoid),
              inc=(sS, 1), waits=[d_const])
        co.op("vector", lambda E_: E_.tensor_scalar_mul(gneg[:], gneg[:], -1.0),
              inc=(sV, 1), waits=[sS])
        v_gneg = sV.n

        with ExitStack() as pg:
            dtd_sb = pg.enter_context(nc.sbuf_tensor("dtd_sb", [128, 128], F32))
            rjt_sb = pg.enter_context(nc.sbuf_tensor("rjt_sb", [128, 8 * 128], F32))
            cjr_sb = pg.enter_context(nc.sbuf_tensor("cjr_sb", [128, 1], F32))
            pstage = pg.enter_context(nc.sbuf_tensor("pstage", [128, E], F32))
            d_pg = S_("d_pg")
            d_pg.h = pg.enter_context(nc.semaphore("d_pg"))
            co.op("sync", lambda E_: E_.dma_start(out=dtd_sb[:], in_=DTD),
                  inc=(d_pg, 16), dma=True)
            co.op("sync", lambda E_: E_.dma_start(out=rjt_sb[:], in_=ROWJT8),
                  inc=(d_pg, 16), dma=True)
            co.op("sync", lambda E_: E_.dma_start(out=cjr_sb[:], in_=COLJR),
                  inc=(d_pg, 16), dma=True)
            pg_full = d_pg.n
            for oc in range(OC):
                co.op("vector", lambda E_, o=oc: E_.tensor_copy(
                    gcol[0:64, o:o + 1], gneg[0:64, 2 * o:2 * o + 1]),
                    inc=(sV, 1), waits=[(sV, v_gneg)])
                co.op("vector", lambda E_, o=oc: E_.tensor_copy(
                    gcol[64:128, o:o + 1], gneg[64:128, 2 * o + 1:2 * o + 2]),
                    inc=(sV, 1))
            v_gcol = sV.n
            for h in range(H):
                co.op("scalar", lambda E_, hh=h: E_.activation(
                    mdiag[:, hh * 128:(hh + 1) * 128], dtd_sb[:], AF.Exp,
                    scale=gneg[:, hh:hh + 1]),
                    inc=(sS, 1), waits=[(d_pg, pg_full), (sV, v_gneg)])
            co.op("vector", lambda E_: E_.tensor_copy(
                pstage[:].rearrange("p (a b) -> p a b", b=128),
                gcol[:].rearrange("p (a b) -> p a b", b=1)
                .broadcast_to([128, OC, 128])),
                inc=(sV, 1), waits=[(sV, v_gcol)])
            co.op("vector", lambda E_: E_.tensor_tensor(
                pstage[:], pstage[:], rjt_sb[:], ALU.mult),
                inc=(sV, 1), waits=[(d_pg, pg_full)])
            vfm = sV.n
            co.op("scalar", lambda E_: E_.activation(decfm[:], pstage[:], AF.Exp),
                  inc=(sS, 1), waits=[(sV, vfm)])
            s_decfm = sS.n
            co.op("vector", lambda E_: E_.tensor_copy(
                pstage[:].rearrange("p (a b) -> p a b", b=64),
                gneg[:].rearrange("p (a b) -> p a b", b=1)
                .broadcast_to([128, H, 64])),
                inc=(sV, 1), waits=[(sS, s_decfm)])
            co.op("vector", lambda E_: E_.tensor_scalar_mul(
                pstage[:], pstage[:], cjr_sb[:]), inc=(sV, 1))
            vbt = sV.n
            co.op("scalar", lambda E_: E_.activation(decBtm[:], pstage[:], AF.Exp),
                  inc=(sS, 1), waits=[(sV, vbt)])
            s_decb = sS.n
            co.op("vector", lambda E_: E_.tensor_copy(
                pstage[:].rearrange("p (a b) -> p a b", b=128),
                gcol[:].rearrange("p (a b) -> p a b", b=1)
                .broadcast_to([128, OC, 128])),
                inc=(sV, 1), waits=[(sS, s_decb)])
            vdc = sV.n
            co.op("scalar", lambda E_: E_.activation(
                dc128[:], pstage[:], AF.Exp, scale=128.0),
                inc=(sS, 1), waits=[(sV, vdc)])
            s_prolog = sS.n
            v_prolog = sV.n
            co.barrier([sS, sV])
            co.flush(blk)

        with ExitStack() as pa:
            WqT_sb = pa.enter_context(nc.sbuf_tensor("WqT_sb", [128, OC * E], BF16))
            WkT_sb = pa.enter_context(nc.sbuf_tensor("WkT_sb", [128, OC * E], BF16))
            WvT_sb = pa.enter_context(nc.sbuf_tensor("WvT_sb", [128, OC * E], BF16))
            WgT_sb = pa.enter_context(nc.sbuf_tensor("WgT_sb", [128, OC * E], BF16))
            bq_sb = pa.enter_context(nc.sbuf_tensor("bq_sb", [128, OC], F32))
            bk_sb = pa.enter_context(nc.sbuf_tensor("bk_sb", [128, OC], F32))
            bv_sb = pa.enter_context(nc.sbuf_tensor("bv_sb", [128, E], F32))
            bg_sb = pa.enter_context(nc.sbuf_tensor("bg_sb", [128, E], F32))
            cq_sb = pa.enter_context(nc.sbuf_tensor("cq_sb", [128, T], F32))
            sq_sb = pa.enter_context(nc.sbuf_tensor("sq_sb", [128, T], F32))
            ck_sb = pa.enter_context(nc.sbuf_tensor("ck_sb", [128, T], F32))
            sk_sb = pa.enter_context(nc.sbuf_tensor("sk_sb", [128, T], F32))
            xst = [pa.enter_context(nc.sbuf_tensor(f"xst{i}", [128, E], BF16))
                   for i in range(2)]
            xfm = pa.enter_context(nc.sbuf_tensor("xfm", [128, OC * TILE], BF16))
            qst = [pa.enter_context(nc.sbuf_tensor(f"qst{i}", [128, TILE], BF16))
                   for i in range(3)]
            tmpA = pa.enter_context(nc.sbuf_tensor("tmpA", [128, TILE], F32))
            tmpB = pa.enter_context(nc.sbuf_tensor("tmpB", [128, TILE], F32))
            qrst = [pa.enter_context(nc.sbuf_tensor(f"qrst{i}", [128, TILE], BF16))
                    for i in range(3)]
            vgpre = pa.enter_context(nc.sbuf_tensor("vgpre", [128, E], F32))
            vst = [pa.enter_context(nc.sbuf_tensor(f"vst{i}", [128, E], BF16))
                   for i in range(2)]
            gst = [pa.enter_context(nc.sbuf_tensor(f"gst{i}", [128, E], F32))
                   for i in range(2)]
            psT = [pa.enter_context(nc.psum_tensor(f"psT{i}", [128, 128], BF16))
                   for i in range(2)]
            pQK = [pa.enter_context(nc.psum_tensor(f"pQK{i}", [128, TILE], F32))
                   for i in range(3)]
            pR = pa.enter_context(nc.psum_tensor("pR", [128, TILE], F32))
            pVG = [pa.enter_context(nc.psum_tensor(f"pVG{i}", [128, 512], F32))
                   for i in range(2)]
            psT_v = [0, 0]

            for src, dst in [(WqT, WqT_sb), (WkT, WkT_sb),
                             (WvT, WvT_sb), (WgT, WgT_sb)]:
                co.op("gpsimd", lambda E_, s_=src, d_=dst: E_.dma_start(
                    out=d_[:].rearrange("p (c f) -> p c f", c=OC),
                    in_=s_.rearrange("(c p) f -> p c f", p=128)),
                    inc=(d_wpg, 16), dma=True)
            for src, dst in [(bq2, bq_sb), (bk2, bk_sb), (bvb, bv_sb),
                             (bgb, bg_sb), (cosq, cq_sb), (sinq, sq_sb),
                             (cosk, ck_sb), (sink, sk_sb)]:
                co.op("sync", lambda E_, s_=src, d_=dst: E_.dma_start(
                    out=d_[:], in_=s_), inc=(d_wp, 16), dma=True)
            wp_full = d_wp.n
            wpg_full = d_wpg.n

            xst_rd = [0, 0]
            xst_ld = [0, 0]
            qst_tok = [(0, 0), (0, 0), (0, 0)]
            qr_dma = [0, 0, 0]
            v_dma = [0, 0]
            g_dma = [0, 0]
            pQK_sv = [0, 0, 0]
            pR_v = 0
            pVG_rd2 = [0, 0]
            vgpre_rd_s = 0

            for gt in range(NTILE):
                s, ti = gt // TPS, gt % TPS
                t0 = ti * TILE
                for tc in range(4):
                    par = tc % 2
                    r0 = s * T + t0 + tc * 128
                    co.op("gpsimd", lambda E_, p=par, r=r0: E_.dma_start(
                        out=xst[p][:], in_=x[r:r + 128, :]),
                        inc=(d_x[par], 16), dma=True,
                        waits=[(sT, xst_rd[par])])
                    xst_ld[par] = d_x[par].n
                    for oc in range(OC):
                        tb = oc % 2
                        last = co.op("tensor", lambda E_, p=par, o=oc, t2=tb:
                                     E_.transpose(
                            psT[t2][:], xst[p][:, o * 128:(o + 1) * 128], id_sb[:]),
                            inc=(sT, 1),
                            waits=[(d_x[par], xst_ld[par]), (sV, psT_v[tb]),
                                   (d_const, 16 * 5)])
                        if oc == OC - 1:
                            xst_rd[par] = last
                        psT_v[tb] = co.op("vector",
                                          lambda E_, o=oc, c=tc, t2=tb: E_.tensor_copy(
                            xfm[:, o * TILE + c * 128: o * TILE + (c + 1) * 128],
                            psT[t2][:]), inc=(sV, 1), waits=[(sT, last)])
                v_xfm = sV.n

                for Wsb, bsb, ctab, stab, spill in [
                        (WqT_sb, bq_sb, cq_sb, sq_sb, Qs),
                        (WkT_sb, bk_sb, ck_sb, sk_sb, Ks)]:
                    for oc in range(OC):
                        rot = oc % 3
                        co.wait("tensor", d_wp, wp_full)
                        co.wait("tensor", d_wpg, wpg_full)
                        co.wait("tensor", sV, v_xfm)
                        co.wait("tensor", sS, pQK_sv[rot])
                        for fc in range(OC):
                            bld = (lambda E_, o=oc, f=fc, W=Wsb, r=rot:
                                   E_.matmul(pQK[r][:],
                                             W[:, f * E + o * 128:f * E + (o + 1) * 128],
                                             xfm[:, f * TILE:(f + 1) * TILE],
                                             start=(f == 0), stop=(f == OC - 1)))
                            if fc == OC - 1:
                                co.op("tensor", bld, inc=(sT, 1))
                            else:
                                co.raw("tensor", bld)
                        t_proj = sT.n
                        sevac = co.op("scalar", lambda E_, o=oc, r=rot, bb=bsb:
                                      E_.activation(qst[r][:], pQK[r][:], AF.Identity,
                                                    bias=bb[:, o:o + 1]),
                                      inc=(sS, 1),
                                      waits=[(sT, t_proj), (sT, qst_tok[rot][0]),
                                             (sV, qst_tok[rot][1]),
                                             (d_wp, wp_full), (d_wpg, wpg_full)])
                        pQK_sv[rot] = sevac
                        tperm = co.op("tensor", lambda E_, r=rot: E_.matmul(
                            pR[:], pm_sb[:], qst[r][:], start=True, stop=True),
                            inc=(sT, 1), waits=[(sS, sevac), (sV, pR_v)])
                        co.op("vector", lambda E_, r=rot, ct=ctab, tz=t0: E_.tensor_tensor(
                            tmpA[:], qst[r][:], ct[:, tz:tz + TILE], ALU.mult),
                            inc=(sV, 1), waits=[(sS, sevac), (d_wp, wp_full), (d_wpg, wpg_full)])
                        vrope1 = co.op("vector", lambda E_, st_=stab, tz=t0: E_.tensor_tensor(
                            tmpB[:], pR[:], st_[:, tz:tz + TILE], ALU.mult),
                            inc=(sV, 1), waits=[(sT, tperm)])
                        pR_v = vrope1
                        vrope2 = co.op("vector", lambda E_, r=rot: E_.tensor_tensor(
                            qrst[r][:], tmpA[:], tmpB[:], ALU.add),
                            inc=(sV, 1),
                            waits=[(d_spq[rot], qr_dma[rot])])
                        qst_tok[rot] = (tperm, vrope1)
                        co.op("sync", lambda E_, r=rot, o=oc, sp=spill, ss=s, tz=t0:
                              E_.dma_start(out=sp[ss, o, :, tz:tz + TILE],
                                           in_=qrst[r][:]),
                              inc=(d_spq[rot], 16), dma=True, waits=[(sV, vrope2)])
                        qr_dma[rot] = d_spq[rot].n

                for tc in range(4):
                    kcg = ti * 4 + tc
                    for which in ("v", "g"):
                        Wsb = WvT_sb if which == "v" else WgT_sb
                        par = tc % 2
                        t_vgh = [0, 0]
                        for half in range(2):
                            co.wait("tensor", sV, pVG_rd2[half])
                            for fc in range(OC):
                                bld = (lambda E_, f=fc, h2=half, c=tc, W=Wsb:
                                       E_.matmul(pVG[h2][:],
                                                 xfm[:, f * TILE + c * 128:
                                                     f * TILE + (c + 1) * 128],
                                                 W[:, f * E + h2 * 512:f * E + (h2 + 1) * 512],
                                                 start=(f == 0), stop=(f == OC - 1)))
                                if fc == OC - 1:
                                    co.op("tensor", bld, inc=(sT, 1))
                                else:
                                    co.raw("tensor", bld)
                            t_vgh[half] = sT.n
                        if which == "v":
                            for half in range(2):
                                vv = co.op("vector", lambda E_, p=par, h2=half:
                                           E_.tensor_tensor(
                                    vst[p][:, h2 * 512:(h2 + 1) * 512], pVG[h2][:],
                                    bv_sb[:, h2 * 512:(h2 + 1) * 512], ALU.add),
                                    inc=(sV, 1),
                                    waits=[(sT, t_vgh[half]),
                                           (d_spv[par], v_dma[par]),
                                           (d_wp, wp_full), (d_wpg, wpg_full)])
                                pVG_rd2[half] = vv
                            co.op("sync", lambda E_, p=par, ss=s, k=kcg: E_.dma_start(
                                out=Vs[ss, k], in_=vst[p][:]),
                                inc=(d_spv[par], 16), dma=True, waits=[(sV, sV.n)])
                            v_dma[par] = d_spv[par].n
                        else:
                            for half in range(2):
                                vg = co.op("vector", lambda E_, h2=half:
                                           E_.tensor_tensor(
                                    vgpre[:, h2 * 512:(h2 + 1) * 512], pVG[h2][:],
                                    bg_sb[:, h2 * 512:(h2 + 1) * 512], ALU.add),
                                    inc=(sV, 1),
                                    waits=[(sT, t_vgh[half]), (sS, vgpre_rd_s),
                                           (d_wp, wp_full), (d_wpg, wpg_full)])
                                pVG_rd2[half] = vg
                            sg = co.op("scalar", lambda E_, p=par: E_.activation(
                                gst[p][:], vgpre[:], AF.Silu),
                                inc=(sS, 1),
                                waits=[(sV, sV.n), (d_spg[par], g_dma[par])])
                            vgpre_rd_s = sg
                            co.op("sync", lambda E_, p=par, ss=s, k=kcg: E_.dma_start(
                                out=Gs[ss, k], in_=gst[p][:]),
                                inc=(d_spg[par], 16), dma=True, waits=[(sS, sg)])
                            g_dma[par] = d_spg[par].n

            co.barrier([sT, sV, sS] + d_spq + d_spv + d_spg)
            co.flush(blk)

        with ExitStack() as pb:
            Q_sb = pb.enter_context(nc.sbuf_tensor("Q_sb", [128, OC * T], BF16))
            K_sb = pb.enter_context(nc.sbuf_tensor("K_sb", [128, OC * T], BF16))
            V_sb = pb.enter_context(nc.sbuf_tensor("V_sb", [128, KC * E], BF16))
            O_sb = pb.enter_context(nc.sbuf_tensor("O_sb", [128, OC * T], BF16))

            for s in range(SEQ_PER_CORE):
                co.op("scalar", lambda E_, ss=s: E_.dma_start(
                    out=Q_sb[:].rearrange("p (o t) -> p o t", o=OC),
                    in_=Qs[ss].rearrange("o p t -> p o t")),
                    inc=(d_qkv, 16), dma=True, waits=d_spq)
                co.op("scalar", lambda E_, ss=s: E_.dma_start(
                    out=K_sb[:].rearrange("p (o t) -> p o t", o=OC),
                    in_=Ks[ss].rearrange("o p t -> p o t")),
                    inc=(d_qkv, 16), dma=True, waits=d_spq)
                co.op("scalar", lambda E_, ss=s: E_.dma_start(
                    out=V_sb[:].rearrange("p (k e) -> p k e", k=KC),
                    in_=Vs[ss].rearrange("k p e -> p k e")),
                    inc=(d_qkv, 16), dma=True, waits=d_spv)
                qkv_full = d_qkv.n

                with ExitStack() as pbp:
                    KT_sb = pbp.enter_context(
                        nc.sbuf_tensor(f"KT_sb{s}", [128, KC * E], BF16))
                    F_bf = pbp.enter_context(
                        nc.sbuf_tensor(f"F_bf{s}", [128, KC * 512], BF16))
                    Qhat = pbp.enter_context(
                        nc.sbuf_tensor(f"Qhat{s}", [128, E], BF16))
                    Khat = pbp.enter_context(
                        nc.sbuf_tensor(f"Khat{s}", [128, E], BF16))
                    Rall = pbp.enter_context(
                        nc.sbuf_tensor(f"Rall{s}", [128, E], F32))
                    Rbf = pbp.enter_context(
                        nc.sbuf_tensor(f"Rbf{s}", [128, E], BF16))
                    PTs = [pbp.enter_context(
                        nc.sbuf_tensor(f"PTs{s}_{i}", [128, 128], BF16))
                        for i in range(4)]

                    with ExitStack() as ptr:
                        psT3 = [ptr.enter_context(
                            nc.psum_tensor(f"psT3{s}_{i}", [128, 128], BF16))
                            for i in range(2)]
                        psT3_v = [0, 0]
                        for tc in range(KC):
                            for oc in range(OC):
                                i2_ = (tc * OC + oc) % 2
                                tt = co.op("tensor", lambda E_, o=oc, k=tc, i3=i2_:
                                           E_.transpose(
                                               psT3[i3][:],
                                               K_sb[:, o * T + k * 128:o * T + (k + 1) * 128],
                                               id_sb[:]),
                                           inc=(sT, 1),
                                           waits=[(d_qkv, qkv_full), (sV, psT3_v[i2_])])
                                psT3_v[i2_] = co.op(
                                    "vector", lambda E_, o=oc, k=tc, i3=i2_:
                                    E_.tensor_copy(
                                        KT_sb[:, k * E + o * 128:k * E + (o + 1) * 128],
                                        psT3[i3][:]),
                                    inc=(sV, 1), waits=[(sT, tt)])
                        v_kt = sV.n

                    with ExitStack() as pfs:
                        psumF = [pfs.enter_context(
                            nc.psum_tensor(f"psumF{s}_{i}", [128, 128], F32))
                            for i in range(4)]
                        fe_tok = [[] for _ in range(4)]
                        for oc in range(OC):
                            fb = oc % 4
                            for J in range(KC - 1, 0, -1):
                                co.wait("tensor", sV, v_kt)
                                for fsem, fcnt in fe_tok[fb]:
                                    co.wait("tensor", fsem, fcnt)
                                tj = co.op("tensor", lambda E_, o=oc, JJ=J, f2=fb:
                                           E_.matmul(
                                    psumF[f2][:],
                                    KT_sb[:, JJ * E + o * 128:JJ * E + (o + 1) * 128],
                                    V_sb[:, JJ * E + o * 128:JJ * E + (o + 1) * 128],
                                    start=(JJ == KC - 1), stop=(JJ == 1),
                                    skip_group_check=True),
                                    inc=(sT, 1))
                                s2_ = co.op("scalar", lambda E_, o=oc, JJ=J, f2=fb:
                                      E_.activation(
                                          F_bf[0:64, (JJ - 1) * 512 + o * 64:
                                               (JJ - 1) * 512 + o * 64 + 64],
                                          psumF[f2][0:64, 0:64], AF.Copy),
                                      inc=(sS, 1), waits=[(sT, tj)])
                                v2 = co.op(
                                    "vector", lambda E_, o=oc, JJ=J, f2=fb:
                                    E_.tensor_copy(
                                        F_bf[64:128, (JJ - 1) * 512 + o * 64:
                                             (JJ - 1) * 512 + o * 64 + 64],
                                        psumF[f2][64:128, 64:128]),
                                    inc=(sV, 1), waits=[(sT, tj)])
                                fe_tok[fb] = [(sS, s2_), (sV, v2)]
                    s_fdone = sS.n
                    vf_done = sV.n
                    co.wait("tensor", sS, s_fdone)
                    co.wait("tensor", sV, vf_done)

                    with ExitStack() as pfw:
                        pSTd = [pfw.enter_context(
                            nc.psum_tensor(f"pSTd{s}_{i}", [128, 128], F32))
                            for i in range(2)]
                        pOutE = [pfw.enter_context(
                            nc.psum_tensor(f"pOutE{s}_{i}", [128, 128], F32))
                            for i in range(2)]
                        pOutO = [pfw.enter_context(
                            nc.psum_tensor(f"pOutO{s}_{i}", [128, 128], F32))
                            for i in range(2)]
                        psumS = [pfw.enter_context(
                            nc.psum_tensor(f"psumS{s}_{i}", [128, 128], F32))
                            for i in range(2)]

                        co.wait("tensor", sS, s_fdone)
                        qhat_rd = 0
                        khat_rd = 0
                        rbf_rd = 0
                        radd_v = [0, 0]
                        pt_rd = [0, 0, 0, 0]
                        pstd_rd = [0, 0]
                        poute_rd = [0, 0]
                        pouto_rd = [0, 0]
                        for I in range(KC):
                            vq = co.op("vector", lambda E_, II=I: E_.tensor_tensor(
                                Qhat[:].rearrange("p (o j) -> p o j", o=OC),
                                Q_sb[:].rearrange("p (o t) -> p o t", o=OC)
                                [:, :, II * 128:(II + 1) * 128],
                                decfm[:].rearrange("p (o j) -> p o j", o=OC),
                                ALU.mult),
                                inc=(sV, 1),
                                waits=[(sT, qhat_rd), (sS, s_decfm),
                                       (d_qkv, qkv_full)])
                            vk = co.op("vector", lambda E_, II=I: E_.tensor_tensor(
                                Khat[:], KT_sb[:, II * E:(II + 1) * E], decBtm[:],
                                ALU.mult),
                                inc=(sV, 1), waits=[(sT, khat_rd), (sS, s_decb),
                                                    (sV, v_kt)])
                            if I > 0:
                                vrbf = co.op("vector", lambda E_: E_.tensor_copy(
                                    Rbf[:], Rall[:]), inc=(sV, 1),
                                    waits=[(sT, rbf_rd)])
                                co.op("vector", lambda E_: E_.tensor_tensor(
                                    Rall[:], Rall[:], dc128[:], ALU.mult),
                                    inc=(sV, 1), waits=[(sS, s_prolog)])
                            else:
                                vrbf = 0
                            for oc in range(OC):
                                sb2 = oc % 2
                                co.wait("tensor", sV, vk)
                                co.wait("tensor", sV, radd_v[sb2])
                                ts2 = co.op("tensor", lambda E_, o=oc, II=I, b3=sb2:
                                            E_.matmul(
                                    psumS[b3][:],
                                    Khat[:, o * 128:(o + 1) * 128],
                                    V_sb[:, II * E + o * 128:II * E + (o + 1) * 128],
                                    start=True, stop=True,
                                    skip_group_check=True),
                                    inc=(sT, 1))
                                if I == 0:
                                    radd_v[sb2] = co.op(
                                        "vector", lambda E_, o=oc, b3=sb2:
                                        E_.tensor_copy(
                                            Rall[:, o * 128:(o + 1) * 128],
                                            psumS[b3][:]),
                                        inc=(sV, 1), waits=[(sT, ts2)])
                                else:
                                    radd_v[sb2] = co.op(
                                        "vector", lambda E_, o=oc, b3=sb2:
                                        E_.tensor_tensor(
                                            Rall[:, o * 128:(o + 1) * 128],
                                            Rall[:, o * 128:(o + 1) * 128],
                                            psumS[b3][:], ALU.add),
                                        inc=(sV, 1), waits=[(sT, ts2)])
                            for oc in range(OC):
                                h0, h1 = 2 * oc, 2 * oc + 1
                                pe, po = (2 * oc) % 4, (2 * oc + 1) % 4
                                r5 = oc % 2
                                tse = co.op("tensor", lambda E_, o=oc, II=I:
                                            E_.matmul(
                                    pSTd[0][:],
                                    K_sb[0:64, o * T + II * 128:o * T + (II + 1) * 128],
                                    Q_sb[0:64, o * T + II * 128:o * T + (II + 1) * 128],
                                    start=True, stop=True, skip_group_check=True),
                                    inc=(sT, 1), waits=[(sV, pstd_rd[0])])
                                tso = co.op("tensor", lambda E_, o=oc, II=I:
                                            E_.matmul(
                                    pSTd[1][:],
                                    K_sb[64:128, o * T + II * 128:o * T + (II + 1) * 128],
                                    Q_sb[64:128, o * T + II * 128:o * T + (II + 1) * 128],
                                    start=True, stop=True, skip_group_check=True),
                                    inc=(sT, 1), waits=[(sV, pstd_rd[1])])
                                vpe = co.op("vector", lambda E_, hh=h0, p4=pe:
                                            E_.scalar_tensor_tensor(
                                    PTs[p4][:], mdiag[:, hh * 128:(hh + 1) * 128],
                                    1.0, pSTd[0][:], ALU.min, ALU.mult),
                                    inc=(sV, 1),
                                    waits=[(sT, tse), (sT, pt_rd[pe])])
                                pstd_rd[0] = vpe
                                vpo = co.op("vector", lambda E_, hh=h1, p4=po:
                                            E_.scalar_tensor_tensor(
                                    PTs[p4][:], mdiag[:, hh * 128:(hh + 1) * 128],
                                    1.0, pSTd[1][:], ALU.min, ALU.mult),
                                    inc=(sV, 1),
                                    waits=[(sT, tso), (sT, pt_rd[po])])
                                pstd_rd[1] = vpo
                                for par2, hh, p4, ptile, prd in (
                                        (0, h0, pe, pOutE, poute_rd),
                                        (64, h1, po, pOutO, pouto_rd)):
                                    co.wait("tensor", sS, prd[r5])
                                    co.wait("tensor", sV, vpo if par2 else vpe)
                                    mm_list = []
                                    mm_list.append(
                                        lambda E_, b4=par2, h5=hh, p5=p4, r6=r5,
                                        II=I, pt2=ptile: E_.matmul(
                                            pt2[r6][b4:b4 + 64, :],
                                            V_sb[:, II * E + h5 * 64:II * E + h5 * 64 + 64],
                                            PTs[p5][:], start=True, stop=False,
                                            tile_position=(0, b4),
                                            skip_group_check=True))
                                    if I > 0:
                                        co.wait("tensor", sV, vrbf)
                                        mm_list.append(
                                            lambda E_, b4=par2, o=oc, r6=r5,
                                            II=I, pt2=ptile: E_.matmul(
                                                pt2[r6][b4:b4 + 64, :],
                                                Rbf[b4:b4 + 64,
                                                    o * 128 + b4:o * 128 + b4 + 64],
                                                Qhat[b4:b4 + 64,
                                                     o * 128:(o + 1) * 128],
                                                start=False, stop=False,
                                                tile_position=(b4, b4),
                                                skip_group_check=True))
                                        co.wait("tensor", sV, vq)
                                    if I < KC - 1:
                                        co.wait("tensor", sS, s_fdone)
                                        mm_list.append(
                                            lambda E_, b4=par2, o=oc, r6=r5,
                                            II=I, pt2=ptile: E_.matmul(
                                                pt2[r6][b4:b4 + 64, :],
                                                F_bf[b4:b4 + 64,
                                                     II * 512 + o * 64:II * 512 + o * 64 + 64],
                                                Q_sb[b4:b4 + 64,
                                                     o * T + II * 128:o * T + (II + 1) * 128],
                                                start=False, stop=False,
                                                tile_position=(b4, b4),
                                                skip_group_check=True))
                                    for mi, bld in enumerate(mm_list):
                                        if mi == len(mm_list) - 1:
                                            co.op("tensor", bld, inc=(sT, 1))
                                        else:
                                            co.raw("tensor", bld)
                                    t_out = sT.n
                                    pt_rd[p4] = t_out
                                    if I > 0:
                                        rbf_rd = t_out
                                    qhat_rd = t_out
                                    prd[r5] = co.op(
                                        "scalar", lambda E_, b4=par2, o=oc, II=I,
                                        r6=r5, pt2=ptile: E_.activation(
                                            O_sb[b4:b4 + 64,
                                                 o * T + II * 128:o * T + (II + 1) * 128],
                                            pt2[r6][b4:b4 + 64, :], AF.Copy),
                                        inc=(sS, 1), waits=[(sT, t_out)])
                            khat_rd = sT.n
                    v_heads_done = sV.n
                    s_heads_done = sS.n
                    co.barrier([sT, sV, sS])
                    co.flush(blk)

                with ExitStack() as pc:
                    Gld = [pc.enter_context(nc.sbuf_tensor(f"Gld{s}_{i}", [128, E], F32))
                           for i in range(2)]
                    Xld = [pc.enter_context(nc.sbuf_tensor(f"Xld{s}_{i}", [128, E], F32))
                           for i in range(2)]
                    Otm = pc.enter_context(nc.sbuf_tensor(f"Otm{s}", [128, E], F32))
                    Rr = pc.enter_context(nc.sbuf_tensor(f"Rr{s}", [128, E], F32))
                    sqb = pc.enter_context(nc.sbuf_tensor(f"sqb{s}", [128, E], F32))
                    X1tm = [pc.enter_context(nc.sbuf_tensor(f"X1tm{s}_{i}", [128, E], F32))
                            for i in range(2)]
                    X1bfs = pc.enter_context(nc.sbuf_tensor(f"X1bfs{s}", [128, E], BF16))
                    Xfms = [pc.enter_context(
                        nc.sbuf_tensor(f"Xfms{s}_{i}", [128, OC * TILE], BF16))
                        for i in range(2)]
                    stat = pc.enter_context(nc.sbuf_tensor(f"stat{s}", [128, 8], F32))
                    logit = pc.enter_context(nc.sbuf_tensor(f"logit{s}", [128, NE], F32))
                    mk1 = pc.enter_context(nc.sbuf_tensor(f"mk1{s}", [128, NE], F32))
                    mbf = pc.enter_context(nc.sbuf_tensor(f"mbf{s}", [128, NE], BF16))
                    mT_sb = pc.enter_context(nc.sbuf_tensor(f"mT_sb{s}", [8, 128], BF16))
                    Yst = pc.enter_context(nc.sbuf_tensor(f"Yst{s}", [128, E], F32))
                    psT2 = [pc.enter_context(
                        nc.psum_tensor(f"psT2{s}_{i}", [128, 128], BF16))
                        for i in range(2)]
                    pL = pc.enter_context(nc.psum_tensor(f"pL{s}", [128, NE], F32))
                    pMT = pc.enter_context(nc.psum_tensor(f"pMT{s}", [8, 128], BF16))
                    pSel = [pc.enter_context(
                        nc.psum_tensor(f"pSel{s}_{i}", [128, 512], F32)) for i in range(2)]

                    g_ld = [0, 0]
                    x_ld = [0, 0]
                    g_rd = [0, 0]
                    x_rd = [0, 0]
                    psT2_v = [0, 0]
                    x1tm_rd = [0, 0]
                    x1tm_dma = [0, 0]
                    xfms_dma = [0, 0]
                    xfms_rd = [0, 0]
                    ys_dma = 0

                    for tc in range(KC):
                        par = tc % 2
                        gt = s * TPS + tc // 4
                        tcw = tc % 4
                        fpar = (tc // 4) % 2
                        r0 = s * T + tc * 128
                        co.op("scalar", lambda E_, p=par, ss=s, k=tc: E_.dma_start(
                            out=Gld[p][:], in_=Gs[ss, k]),
                            inc=(d_g[par], 16), dma=True,
                            waits=[(sV, g_rd[par])] + d_spg)
                        g_ld[par] = d_g[par].n
                        co.op("sync", lambda E_, p=par, r=r0: E_.dma_start(
                            out=Xld[p][:], in_=x[r:r + 128, :]),
                            inc=(d_xf[par], 16), dma=True, waits=[(sV, x_rd[par])])
                        x_ld[par] = d_xf[par].n
                        for oc in range(OC):
                            tb = oc % 2
                            tt = co.op("tensor", lambda E_, o=oc, k=tc, t2=tb:
                                       E_.transpose(
                                psT2[t2][:], O_sb[:, o * T + k * 128:o * T + (k + 1) * 128],
                                id_sb[:]),
                                inc=(sT, 1), waits=[(sV, psT2_v[tb]), (sV, v_heads_done),
                                                    (sS, s_heads_done)])
                            psT2_v[tb] = co.op("vector",
                                               lambda E_, o=oc, t2=tb: E_.tensor_copy(
                                Otm[:, o * 128:(o + 1) * 128], psT2[t2][:]),
                                inc=(sV, 1), waits=[(sT, tt)])
                        co.op("vector", lambda E_, p=par: E_.tensor_tensor(
                            Otm[:], Otm[:], Gld[p][:], ALU.mult),
                            inc=(sV, 1), waits=[(d_g[par], g_ld[par])])
                        g_rd[par] = sV.n
                        co.op("vector", lambda E_, p=par: E_.tensor_tensor(
                            Rr[:], Otm[:], Xld[p][:], ALU.add),
                            inc=(sV, 1), waits=[(d_xf[par], x_ld[par])])
                        x_rd[par] = sV.n
                        co.op("vector", lambda E_: E_.reduce_sum(
                            stat[:, 0:1], Rr[:], axis=AX.X), inc=(sV, 1))
                        co.op("vector", lambda E_: E_.tensor_scalar_mul(
                            stat[:, 1:2], stat[:, 0:1], 1.0 / E), inc=(sV, 1))
                        co.op("vector", lambda E_: E_.tensor_tensor(
                            sqb[:], Rr[:], Rr[:], ALU.mult), inc=(sV, 1))
                        co.op("vector", lambda E_: E_.reduce_sum(
                            stat[:, 2:3], sqb[:], axis=AX.X), inc=(sV, 1))
                        co.op("vector", lambda E_: E_.tensor_tensor(
                            stat[:, 3:4], stat[:, 1:2], stat[:, 1:2], ALU.mult),
                            inc=(sV, 1))
                        co.op("vector", lambda E_: E_.scalar_tensor_tensor(
                            stat[:, 4:5], stat[:, 2:3], 1.0 / E, stat[:, 3:4],
                            ALU.mult, ALU.subtract), inc=(sV, 1))
                        vvar = co.op("vector", lambda E_: E_.tensor_scalar_add(
                            stat[:, 5:6], stat[:, 4:5], 1e-5), inc=(sV, 1))
                        co.op("scalar", lambda E_: E_.activation(
                            stat[:, 6:7], stat[:, 5:6], AF.Ln), inc=(sS, 1),
                            waits=[(sV, vvar)])
                        ssd = co.op("scalar", lambda E_: E_.activation(
                            stat[:, 7:8], stat[:, 6:7], AF.Exp, scale=-0.5),
                            inc=(sS, 1))
                        co.wait("vector", sS, ssd)
                        vx1 = co.op("vector", lambda E_, p=par: E_.tensor_scalar(
                            X1tm[p][:], Rr[:], stat[:, 1:2], stat[:, 7:8],
                            ALU.subtract, ALU.mult), inc=(sV, 1),
                            waits=[(d_x1f[par], x1tm_dma[par]), (sV, x1tm_rd[par])])
                        co.op("sync", lambda E_, p=par, r=r0 // 128: E_.dma_start(
                            out=X1f[r], in_=X1tm[p][:]),
                            inc=(d_x1f[par], 16), dma=True, waits=[(sV, vx1)])
                        x1tm_dma[par] = d_x1f[par].n
                        vbf = co.op("vector", lambda E_, p=par: E_.tensor_copy(
                            X1bfs[:], X1tm[p][:]), inc=(sV, 1))
                        for oc in range(OC):
                            tb = oc % 2
                            tt = co.op("tensor", lambda E_, o=oc, t2=tb: E_.transpose(
                                psT2[t2][:], X1bfs[:, o * 128:(o + 1) * 128], id_sb[:]),
                                inc=(sT, 1),
                                waits=[(sV, vbf), (sV, psT2_v[tb]),
                                       (d_x1b[fpar], xfms_dma[fpar]),
                                       (sT, xfms_rd[fpar])])
                            psT2_v[tb] = co.op("vector",
                                           lambda E_, o=oc, w=tcw, f=fpar, t2=tb:
                                           E_.tensor_copy(
                                               Xfms[f][:, o * TILE + w * 128:
                                                       o * TILE + (w + 1) * 128],
                                               psT2[t2][:]),
                                           inc=(sV, 1), waits=[(sT, tt)])
                        v_xfms = sV.n
                        co.wait("tensor", sV, v_xfms)
                        co.wait("tensor", d_const)
                        co.wait("tensor", d_constg)
                        for oc in range(OC):
                            bld = (lambda E_, o=oc, w=tcw, f=fpar: E_.matmul(
                                pL[:],
                                Xfms[f][:, o * TILE + w * 128:o * TILE + (w + 1) * 128],
                                Wgf_sb[:, o * NE:(o + 1) * NE],
                                start=(o == 0), stop=(o == OC - 1)))
                            if oc == OC - 1:
                                co.op("tensor", bld, inc=(sT, 1))
                            else:
                                co.raw("tensor", bld)
                        t_logit = sT.n
                        co.op("vector", lambda E_: E_.tensor_tensor(
                            logit[:], pL[:], bgf_sb[:], ALU.add),
                            inc=(sV, 1), waits=[(sT, t_logit), d_const])
                        vnm = co.op("vector", lambda E_: E_.reduce_max(
                            stat[:, 0:1], logit[:], axis=AX.X, negate=True),
                            inc=(sV, 1))
                        sex = co.op("scalar", lambda E_: E_.activation(
                            mk1[:], logit[:], AF.Exp, bias=stat[:, 0:1]),
                            inc=(sS, 1), waits=[(sV, vnm)])
                        co.op("vector", lambda E_: E_.reduce_sum(
                            stat[:, 1:2], mk1[:], axis=AX.X),
                            inc=(sV, 1), waits=[(sS, sex)])
                        co.op("vector", lambda E_: E_.reciprocal(
                            stat[:, 2:3], stat[:, 1:2]), inc=(sV, 1))
                        co.op("vector", lambda E_: E_.tensor_scalar(
                            mk1[:], logit[:], stat[:, 0:1], 0.0, ALU.add, ALU.is_ge),
                            inc=(sV, 1))
                        vmw = co.op("vector", lambda E_, g=gt, w=tcw: E_.tensor_scalar(
                            maskW[:, (g * 4 + w) * NE:(g * 4 + w + 1) * NE],
                            mk1[:], stat[:, 2:3], None, ALU.mult),
                            inc=(sV, 1))
                        vmb = co.op("vector", lambda E_, g=gt, w=tcw: E_.tensor_copy(
                            mbf[:], maskW[:, (g * 4 + w) * NE:(g * 4 + w + 1) * NE]),
                            inc=(sV, 1))
                        tmt = co.op("tensor", lambda E_: E_.transpose(
                            pMT[:], mbf[:], id_sb[:]),
                            inc=(sT, 1), waits=[(sV, vmb)])
                        vmt = co.op("vector", lambda E_: E_.tensor_copy(
                            mT_sb[:], pMT[:]), inc=(sV, 1), waits=[(sT, tmt)])
                        for half in range(2):
                            co.op("tensor", lambda E_, hh=half: E_.matmul(
                                pSel[hh][:], mT_sb[:],
                                b2bf[:, hh * 512:(hh + 1) * 512],
                                start=True, stop=True),
                                inc=(sT, 1), waits=[(sV, vmt), d_const, d_constg])
                        t_sel = sT.n
                        for half in range(2):
                            co.op("vector", lambda E_, hh=half, p=par:
                                  E_.tensor_tensor(
                                      Yst[:, hh * 512:(hh + 1) * 512], pSel[hh][:],
                                      X1tm[p][:, hh * 512:(hh + 1) * 512], ALU.add),
                                  inc=(sV, 1),
                                  waits=[(sT, t_sel), (d_ysi, ys_dma)])
                        vys = sV.n
                        x1tm_rd[par] = vys
                        co.op("sync", lambda E_, r=r0 // 128: E_.dma_start(
                            out=Ys[r], in_=Yst[:]),
                            inc=(d_ysi, 16), dma=True, waits=[(sV, vys)])
                        ys_dma = d_ysi.n
                        if tcw == 3:
                            co.op("sync", lambda E_, g=gt, f=fpar: E_.dma_start(
                                out=X1bf[g].rearrange("o p t -> p o t"),
                                in_=Xfms[f][:].rearrange("p (o t) -> p o t", o=OC)),
                                inc=(d_x1b[fpar], 16), dma=True, waits=[(sV, v_xfms)])
                            xfms_dma[fpar] = d_x1b[fpar].n
                            xfms_rd[fpar] = sT.n
                    co.barrier([sT, sV, sS, d_ysi] + d_x1f + d_x1b)
                    co.flush(blk)

        with ExitStack() as pd_:
            W1h = [pd_.enter_context(
                nc.sbuf_tensor(f"W1h{i}", [128, OC * (FH // 2)], BF16))
                for i in range(2)]
            W2h = [pd_.enter_context(
                nc.sbuf_tensor(f"W2h{i}", [128, HC_HALF * E], BF16))
                for i in range(2)]
            X1t = [pd_.enter_context(
                nc.sbuf_tensor(f"X1t{i}", [128, OC * TILE], BF16))
                for i in range(2)]
            H_sb = pd_.enter_context(
                nc.sbuf_tensor("H_sb", [128, HC_HALF * TILE], BF16))
            Ost = [pd_.enter_context(nc.sbuf_tensor(f"Ost{i}", [128, E], F32))
                   for i in range(3)]
            pH = [pd_.enter_context(nc.psum_tensor(f"pH{i}", [128, TILE], F32))
                  for i in range(2)]
            pO = [pd_.enter_context(nc.psum_tensor(f"pO{i}", [128, 512], F32))
                  for i in range(4)]

            w1_ld = [0, 0]
            w2_ld = [0, 0]
            x1_ld = [0, 0]
            x1_rd = [0, 0]
            pH_rd = [0, 0]
            pO_rd = [0, 0, 0, 0]
            ost_dma = [0, 0, 0]
            ost_cyc = 0

            def load_whalf(e, half):
                par = (e * 2 + half) % 2
                cur = sT.n
                co.op("gpsimd", lambda E_, ee=e, hh=half, p=par: E_.dma_start(
                    out=W1h[p][:].rearrange("p (c f) -> p c f", c=OC),
                    in_=W1[ee, :, hh * (FH // 2):(hh + 1) * (FH // 2)]
                    .rearrange("(c p) f -> p c f", p=128)),
                    inc=(d_w1[par], 16), dma=True, waits=[(sT, cur)])
                w1_ld[par] = d_w1[par].n
                co.op("gpsimd", lambda E_, ee=e, hh=half, p=par: E_.dma_start(
                    out=W2h[p][:].rearrange("p (c f) -> p c f", c=HC_HALF),
                    in_=W2[ee, hh * (FH // 2):(hh + 1) * (FH // 2), :]
                    .rearrange("(c p) f -> p c f", p=128)),
                    inc=(d_w2[par], 16), dma=True, waits=[(sT, cur)])
                w2_ld[par] = d_w2[par].n

            load_whalf(0, 0)
            load_whalf(0, 1)
            for e in range(NE):
                for half in range(2):
                    par = (e * 2 + half) % 2
                    for gt in range(NTILE):
                        xpar = gt % 2
                        co.op("scalar", lambda E_, g=gt, p=xpar: E_.dma_start(
                            out=X1t[p][:].rearrange("p (o t) -> p o t", o=OC),
                            in_=X1bf[g].rearrange("o p t -> p o t")),
                            inc=(d_x1t[xpar], 16), dma=True,
                            waits=[(sT, x1_rd[xpar])] + d_x1b)
                        x1_ld[xpar] = d_x1t[xpar].n
                        co.wait("tensor", d_w1[par], w1_ld[par])
                        co.wait("tensor", d_x1t[xpar], x1_ld[xpar])
                        for hc in range(HC_HALF):
                            hrot = hc % 2
                            co.wait("tensor", sS, pH_rd[hrot])
                            for fc in range(OC):
                                bld = (lambda E_, f=fc, c=hc, p=par, xp=xpar:
                                       E_.matmul(
                                           pH[c % 2][:],
                                           W1h[p][:, f * (FH // 2) + c * 128:
                                                  f * (FH // 2) + (c + 1) * 128],
                                           X1t[xp][:, f * TILE:(f + 1) * TILE],
                                           start=(f == 0), stop=(f == OC - 1)))
                                if fc == OC - 1:
                                    co.op("tensor", bld, inc=(sT, 1))
                                else:
                                    co.raw("tensor", bld)
                            t_h = sT.n
                            pH_rd[hrot] = co.op(
                                "scalar", lambda E_, c=hc, ee=e, hh=half:
                                E_.activation(
                                    H_sb[:, c * TILE:(c + 1) * TILE], pH[c % 2][:],
                                    AF.Silu,
                                    bias=b1_sb[:, ee * 32 + hh * 16 + c:
                                               ee * 32 + hh * 16 + c + 1]),
                                inc=(sS, 1), waits=[(sT, t_h), d_const])
                        s_h_done = sS.n
                        x1_rd[xpar] = sT.n
                        co.wait("tensor", sS, s_h_done)
                        co.wait("tensor", d_w2[par], w2_ld[par])
                        for w in range(4):
                            ocy = ost_cyc % 3
                            for foh in range(2):
                                ob = (w * 2 + foh) % 4
                                co.wait("tensor", sV, pO_rd[ob])
                                for hc in range(HC_HALF):
                                    bld = (lambda E_, c=hc, ww=w, f=foh, p=par, b_=ob:
                                           E_.matmul(
                                               pO[b_][:],
                                               H_sb[:, c * TILE + ww * 128:
                                                    c * TILE + (ww + 1) * 128],
                                               W2h[p][:, c * E + f * 512:
                                                      c * E + (f + 1) * 512],
                                               start=(c == 0), stop=(c == HC_HALF - 1)))
                                    if hc == HC_HALF - 1:
                                        co.op("tensor", bld, inc=(sT, 1))
                                    else:
                                        co.raw("tensor", bld)
                                t_o = sT.n
                                pO_rd[ob] = co.op(
                                    "vector", lambda E_, b_=ob, f=foh, g=gt, ww=w,
                                    ee=e, oc_=ocy: E_.tensor_scalar_mul(
                                        Ost[oc_][:, f * 512:(f + 1) * 512], pO[b_][:],
                                        maskW[:, (g * 4 + ww) * NE + ee:
                                              (g * 4 + ww) * NE + ee + 1]),
                                    inc=(sV, 1),
                                    waits=[(sT, t_o), (d_yacc[ocy], ost_dma[ocy])])
                            v_o = sV.n
                            co.op("gpsimd", lambda E_, g=gt, ww=w, oc_=ocy:
                                  E_.dma_start(out=Ys[g * 4 + ww], in_=Ost[oc_][:],
                                               accum_op=ALU.add),
                                  inc=(d_yacc[ocy], 16), dma=True,
                                  waits=[(sV, v_o), d_ysi])
                            ost_dma[ocy] = d_yacc[ocy].n
                            ost_cyc += 1
                    nxt = e * 2 + half + 2
                    if nxt < NE * 2:
                        load_whalf(nxt // 2, nxt % 2)
            co.barrier([sT, sV, sS] + d_yacc)
            co.flush(blk)

        with ExitStack() as pe:
            Yld = [pe.enter_context(nc.sbuf_tensor(f"Yld{i}", [128, E], F32))
                   for i in range(2)]
            sqe = pe.enter_context(nc.sbuf_tensor("sqe", [128, E], F32))
            out_s = [pe.enter_context(nc.sbuf_tensor(f"out_s{i}", [128, E], F32))
                     for i in range(2)]
            stat2 = pe.enter_context(nc.sbuf_tensor("stat2", [128, 8], F32))

            y_ld = [0, 0]
            y_rd = [0, 0]
            o_dma = [0, 0]
            yacc_fulls = [(dy, dy.n) for dy in d_yacc]
            for r in range(NTOK // 128):
                par = r % 2
                co.op("scalar", lambda E_, rr=r, p=par: E_.dma_start(
                    out=Yld[p][:], in_=Ys[rr]),
                    inc=(d_ye[par], 16), dma=True,
                    waits=yacc_fulls + [(sV, y_rd[par])])
                y_ld[par] = d_ye[par].n
                co.op("vector", lambda E_, p=par: E_.reduce_sum(
                    stat2[:, 0:1], Yld[p][:], axis=AX.X),
                    inc=(sV, 1), waits=[(d_ye[par], y_ld[par])])
                co.op("vector", lambda E_: E_.tensor_scalar_mul(
                    stat2[:, 1:2], stat2[:, 0:1], 1.0 / E), inc=(sV, 1))
                co.op("vector", lambda E_, p=par: E_.tensor_tensor(
                    sqe[:], Yld[p][:], Yld[p][:], ALU.mult), inc=(sV, 1))
                co.op("vector", lambda E_: E_.reduce_sum(
                    stat2[:, 2:3], sqe[:], axis=AX.X), inc=(sV, 1))
                co.op("vector", lambda E_: E_.tensor_tensor(
                    stat2[:, 3:4], stat2[:, 1:2], stat2[:, 1:2], ALU.mult),
                    inc=(sV, 1))
                co.op("vector", lambda E_: E_.scalar_tensor_tensor(
                    stat2[:, 4:5], stat2[:, 2:3], 1.0 / E, stat2[:, 3:4],
                    ALU.mult, ALU.subtract), inc=(sV, 1))
                vv = co.op("vector", lambda E_: E_.tensor_scalar_add(
                    stat2[:, 5:6], stat2[:, 4:5], 1e-5), inc=(sV, 1))
                co.op("scalar", lambda E_: E_.activation(
                    stat2[:, 6:7], stat2[:, 5:6], AF.Ln), inc=(sS, 1),
                    waits=[(sV, vv)])
                ss2 = co.op("scalar", lambda E_: E_.activation(
                    stat2[:, 7:8], stat2[:, 6:7], AF.Exp, scale=-0.5),
                    inc=(sS, 1))
                co.wait("vector", sS, ss2)
                vout = co.op("vector", lambda E_, p=par: E_.tensor_scalar(
                    out_s[p][:], Yld[p][:], stat2[:, 1:2], stat2[:, 7:8],
                    ALU.subtract, ALU.mult), inc=(sV, 1),
                    waits=[(d_out[par], o_dma[par])])
                y_rd[par] = vout
                co.op("sync", lambda E_, rr=r, p=par: E_.dma_start(
                    out=y[rr * 128:(rr + 1) * 128, :], in_=out_s[p][:]),
                    inc=(d_out[par], 16), dma=True, waits=[(sV, vout)])
                o_dma[par] = d_out[par].n
            co.raw("sync", lambda E_, v=d_out[0].n: E_.wait_ge(d_out[0].h, v))
            co.raw("sync", lambda E_, v=d_out[1].n: E_.wait_ge(d_out[1].h, v))
            co.flush(blk)

    return nc


def kernel(**inputs):
    per_core, sh = host_prep(inputs)
    nc = build_nc()
    in_maps = []
    for c in range(NCORES):
        m = dict(sh)
        m.update(per_core[c])
        in_maps.append(m)
    res = run_bass_kernel_spmd(nc, in_maps, core_ids=list(range(NCORES)))
    out = np.concatenate(
        [np.asarray(res.results[c]["y"], dtype=np.float32).reshape(
            SEQ_PER_CORE, T, E) for c in range(NCORES)], axis=0)
    return out


# version 09
# speedup vs baseline: 1.0764x; 1.0764x over previous
import numpy as np
import ml_dtypes
import concourse.bass as bass
import concourse.mybir as mybir
from concourse.bass_utils import run_bass_kernel_spmd
from contextlib import ExitStack

F32 = mybir.dt.float32
BF16 = mybir.dt.bfloat16
AF = mybir.ActivationFunctionType
ALU = mybir.AluOpType
AX = mybir.AxisListType

NCORES = 8
B, T, E, H, D = 16, 2048, 1024, 16, 64
SEQ_PER_CORE = B // NCORES
NTOK = SEQ_PER_CORE * T
FH = 4096
NE = 8
OC = E // 128
TILE = 512
NTILE = NTOK // TILE
TPS = T // TILE
KC = T // 128
QT = T // TILE
HC_HALF = 16
BF = ml_dtypes.bfloat16


class Sem:
    def __init__(self, name):
        self.name = name
        self.n = 0
        self.h = None

    def inc(self, k=1):
        self.n += k
        return self.n


class Core:
    ENGS = ("tensor", "vector", "scalar", "sync", "gpsimd")

    def __init__(self):
        self.items = {e: [] for e in self.ENGS}
        self.waited = {e: {} for e in self.ENGS}
        self.selfsem = {}

    def wait(self, eng, sem, val=None):
        v = sem.n if val is None else val
        if v <= 0:
            return
        w = self.waited[eng]
        if w.get(sem, -1) >= v:
            return
        w[sem] = v
        self.items[eng].append(lambda Eg, s=sem, vv=v: Eg.wait_ge(s.h, vv))

    def op(self, eng, build, inc=None, waits=(), dma=False):
        ss = self.selfsem.get(eng)
        if ss is not None and not dma:
            self.wait(eng, ss, ss.n)
        for wspec in waits:
            if isinstance(wspec, Sem):
                self.wait(eng, wspec)
            else:
                self.wait(eng, wspec[0], wspec[1])
        if inc is None:
            self.items[eng].append(build)
            return None
        sem, k = inc
        target = sem.inc(k)
        self.items[eng].append(
            lambda Eg, b=build, s2=sem, kk=k: b(Eg).then_inc(s2.h, kk))
        return target

    def raw(self, eng, f):
        self.items[eng].append(f)

    def barrier(self, sems):
        for eng in self.ENGS:
            for s in sems:
                self.wait(eng, s)

    def flush(self, block):
        items = {e: list(v) for e, v in self.items.items()}

        def replay(lst):
            def f(Eg):
                for it in lst:
                    it(Eg)
            return f
        for name, meth in (("tensor", block.tensor), ("vector", block.vector),
                           ("scalar", block.scalar), ("sync", block.sync),
                           ("gpsimd", block.gpsimd)):
            if items[name]:
                meth(replay(items[name]))
        for e in self.ENGS:
            self.items[e] = []


def host_prep(inputs):
    x = np.ascontiguousarray(inputs["L0_attn_x"], dtype=np.float32)
    per_core = []
    for c in range(NCORES):
        xc = np.ascontiguousarray(
            x[c * SEQ_PER_CORE:(c + 1) * SEQ_PER_CORE].reshape(NTOK, E))
        per_core.append({"x": xc})

    sh = {}
    sh["WqT"] = np.ascontiguousarray(inputs["L0_attn_Wq"].T.astype(np.float32))
    sh["WkT"] = np.ascontiguousarray(inputs["L0_attn_Wk"].T.astype(np.float32))
    sh["WvT"] = np.ascontiguousarray(inputs["L0_attn_Wv"].T.astype(np.float32))
    sh["WgT"] = np.ascontiguousarray(inputs["L0_attn_Wg"].T.astype(np.float32))
    sh["bq2"] = np.ascontiguousarray(
        inputs["L0_attn_bq"].astype(np.float32).reshape(OC, 128).T)
    sh["bk2"] = np.ascontiguousarray(
        inputs["L0_attn_bk"].astype(np.float32).reshape(OC, 128).T)
    sh["bvb"] = np.ascontiguousarray(
        np.broadcast_to(inputs["L0_attn_bv"].astype(np.float32), (128, E)))
    sh["bgb"] = np.ascontiguousarray(
        np.broadcast_to(inputs["L0_attn_bg"].astype(np.float32), (128, E)))
    sh["gammab"] = np.ascontiguousarray(
        np.broadcast_to(inputs["L0_attn_gamma"].astype(np.float32), (128, H)))
    sh["Wgf"] = np.ascontiguousarray(inputs["L0_ffn_Wg"].astype(np.float32))
    sh["bgfb"] = np.ascontiguousarray(
        np.broadcast_to(inputs["L0_ffn_bg"].astype(np.float32), (128, NE)))
    sh["W1"] = np.ascontiguousarray(inputs["L0_ffn_W1"].astype(np.float32))
    sh["b1r"] = np.ascontiguousarray(
        inputs["L0_ffn_b1"].astype(np.float32).reshape(NE, FH // 128, 128)
        .transpose(0, 2, 1))
    sh["W2"] = np.ascontiguousarray(inputs["L0_ffn_W2"].astype(np.float32))
    sh["b2"] = np.ascontiguousarray(inputs["L0_ffn_b2"].astype(np.float32))

    invf = (10000.0 ** (-np.arange(0, D, 2, dtype=np.float64) / D))
    p = np.arange(128)
    fidx = (p % D) % (D // 2)
    t = np.arange(T, dtype=np.float64)
    ang = t[None, :] * invf[fidx][:, None]
    sh["cosq"] = np.ascontiguousarray((np.cos(ang) / 8.0).astype(np.float32))
    sh["sinq"] = np.ascontiguousarray((np.sin(ang) / 8.0).astype(np.float32))
    sh["cosk"] = np.ascontiguousarray(np.cos(ang).astype(np.float32))
    sh["sink"] = np.ascontiguousarray(np.sin(ang).astype(np.float32))
    pm = np.zeros((128, 128), dtype=np.float32)
    for j in range(64):
        pm[2 * j + 1, 2 * j] = -1.0
        pm[2 * j, 2 * j + 1] = 1.0
    sh["PM"] = pm.astype(BF)
    sh["ident"] = np.eye(128, dtype=BF)
    c = np.arange(2 * T, dtype=np.float32)
    sh["DT"] = np.ascontiguousarray(
        (c[None, :] - float(T - 1)) - p[:, None].astype(np.float32))
    j = np.arange(128, dtype=np.float32)
    sh["DTD"] = np.ascontiguousarray(j[None, :] - j[:, None])
    sh["ROWJT8"] = np.ascontiguousarray(np.tile(j, (128, 8)))
    sh["COLJR"] = np.ascontiguousarray((128.0 - j)[:, None])
    return per_core, sh


def build_nc(debug=()):
    nc = bass.Bass("TRN2", target_bir_lowering=False)

    def din(name, shape, dt=F32):
        return nc.dram_tensor(name, shape, dt, kind="ExternalInput").ap()

    x = din("x", [NTOK, E])
    WqT, WkT = din("WqT", [E, E]), din("WkT", [E, E])
    WvT, WgT = din("WvT", [E, E]), din("WgT", [E, E])
    bq2, bk2 = din("bq2", [128, OC]), din("bk2", [128, OC])
    bvb, bgb = din("bvb", [128, E]), din("bgb", [128, E])
    gammab = din("gammab", [128, H])
    Wgf = din("Wgf", [E, NE])
    bgfb = din("bgfb", [128, NE])
    W1 = din("W1", [NE, E, FH])
    b1r = din("b1r", [NE, 128, FH // 128])
    W2 = din("W2", [NE, FH, E])
    b2 = din("b2", [NE, E])
    cosq, sinq = din("cosq", [128, T]), din("sinq", [128, T])
    cosk, sink = din("cosk", [128, T]), din("sink", [128, T])
    PM = din("PM", [128, 128], BF16)
    ident = din("ident", [128, 128], BF16)
    DTt = din("DT", [128, 2 * T])
    DTD = din("DTD", [128, 128])
    ROWJT8 = din("ROWJT8", [128, 8 * 128])
    COLJR = din("COLJR", [128, 1])

    def scratch(name, shape, dt):
        kind = "ExternalOutput" if name in debug else "Internal"
        return nc.dram_tensor(name, shape, dt, kind=kind).ap()

    Qs = scratch("Qs", [SEQ_PER_CORE, OC, 128, T], BF16)
    Ks = scratch("Ks", [SEQ_PER_CORE, OC, 128, T], BF16)
    Vs = scratch("Vs", [SEQ_PER_CORE, KC, 128, E], BF16)
    Gs = scratch("Gs", [SEQ_PER_CORE, KC, 128, E], F32)
    Ofm = scratch("Ofm", [SEQ_PER_CORE, OC, 128, T], BF16)
    X1f = scratch("X1f", [NTOK // 128, 128, E], F32)
    X1bf = scratch("X1bf", [NTILE, OC, 128, TILE], BF16)
    Ys = scratch("Ys", [NTOK // 128, 128, E], F32)
    y = nc.dram_tensor("y", [NTOK, E], F32, kind="ExternalOutput").ap()

    co = Core()
    sems = {}

    def S_(name):
        s = Sem(name)
        sems[name] = s
        return s

    sT, sV, sS = S_("sT"), S_("sV"), S_("sS")
    co.selfsem = {"vector": sV, "scalar": sS}
    d_const = S_("d_const")
    d_constg = S_("d_constg")
    d_wp = S_("d_wp")
    d_wpg = S_("d_wpg")
    d_x = [S_("d_x0"), S_("d_x1")]
    d_spq = [S_("d_spq0"), S_("d_spq1")]
    d_spv = [S_("d_spv0"), S_("d_spv1")]
    d_spg = [S_("d_spg0"), S_("d_spg1")]
    d_qkv = S_("d_qkv")
    d_dt = S_("d_dt")
    d_g = [S_("d_g0"), S_("d_g1")]
    d_xf = [S_("d_xf0"), S_("d_xf1")]
    d_x1f = [S_("d_x1f0"), S_("d_x1f1")]
    d_x1b = [S_("d_x1b0"), S_("d_x1b1")]
    d_ysi = S_("d_ysi")
    d_w1 = [S_("d_w10"), S_("d_w11")]
    d_w2 = [S_("d_w20"), S_("d_w21")]
    d_x1t = [S_("d_x1t0"), S_("d_x1t1")]
    d_yacc = [S_("d_yacc0"), S_("d_yacc1"), S_("d_yacc2")]
    d_ye = [S_("d_ye0"), S_("d_ye1")]
    d_out = [S_("d_out0"), S_("d_out1")]

    with ExitStack() as ctx:
        blk = ctx.enter_context(nc.Block())
        for s in sems.values():
            s.h = ctx.enter_context(nc.semaphore(s.name))

        b1_sb = ctx.enter_context(nc.sbuf_tensor("b1_sb", [128, NE * (FH // 128)], F32))
        b2bf = ctx.enter_context(nc.sbuf_tensor("b2bf", [NE, E], BF16))
        bgf_sb = ctx.enter_context(nc.sbuf_tensor("bgf_sb", [128, NE], F32))
        Wgf_sb = ctx.enter_context(nc.sbuf_tensor("Wgf_sb", [128, OC * NE], BF16))
        gneg = ctx.enter_context(nc.sbuf_tensor("gneg", [128, H], F32))
        gtmp = ctx.enter_context(nc.sbuf_tensor("gtmp", [128, H], F32))
        maskW = ctx.enter_context(nc.sbuf_tensor("maskW", [128, NTILE * 4 * NE], F32))
        id_sb = ctx.enter_context(nc.sbuf_tensor("id_sb", [128, 128], BF16))
        pm_sb = ctx.enter_context(nc.sbuf_tensor("pm_sb", [128, 128], BF16))
        gcol = ctx.enter_context(nc.sbuf_tensor("gcol", [128, NE], F32))
        mdiag = ctx.enter_context(nc.sbuf_tensor("mdiag", [128, H * 128], F32))
        decfm = ctx.enter_context(nc.sbuf_tensor("decfm", [128, E], BF16))
        decBtm = ctx.enter_context(nc.sbuf_tensor("decBtm", [128, E], BF16))
        dc128 = ctx.enter_context(nc.sbuf_tensor("dc128", [128, E], F32))

        co.op("sync", lambda E_: E_.dma_start(
            out=b1_sb[:].rearrange("p (e h) -> p e h", e=NE),
            in_=b1r.rearrange("e p h -> p e h")), inc=(d_const, 16), dma=True)
        co.op("sync", lambda E_: E_.dma_start(out=bgf_sb[:], in_=bgfb),
              inc=(d_const, 16), dma=True)
        co.op("sync", lambda E_: E_.dma_start(out=id_sb[:], in_=ident),
              inc=(d_const, 16), dma=True)
        co.op("sync", lambda E_: E_.dma_start(out=pm_sb[:], in_=PM),
              inc=(d_const, 16), dma=True)
        co.op("sync", lambda E_: E_.dma_start(out=gtmp[:], in_=gammab),
              inc=(d_const, 16), dma=True)
        co.op("gpsimd", lambda E_: E_.dma_start(out=b2bf[:], in_=b2),
              inc=(d_constg, 16), dma=True)
        co.op("gpsimd", lambda E_: E_.dma_start(
            out=Wgf_sb[:].rearrange("p (o e) -> p o e", o=OC),
            in_=Wgf.rearrange("(o p) e -> p o e", p=128)),
            inc=(d_constg, 16), dma=True)
        co.op("scalar", lambda E_: E_.activation(gneg[:], gtmp[:], AF.Sigmoid),
              inc=(sS, 1), waits=[d_const])
        co.op("vector", lambda E_: E_.tensor_scalar_mul(gneg[:], gneg[:], -1.0),
              inc=(sV, 1), waits=[sS])
        v_gneg = sV.n

        with ExitStack() as pg:
            dtd_sb = pg.enter_context(nc.sbuf_tensor("dtd_sb", [128, 128], F32))
            rjt_sb = pg.enter_context(nc.sbuf_tensor("rjt_sb", [128, 8 * 128], F32))
            cjr_sb = pg.enter_context(nc.sbuf_tensor("cjr_sb", [128, 1], F32))
            pstage = pg.enter_context(nc.sbuf_tensor("pstage", [128, E], F32))
            d_pg = S_("d_pg")
            d_pg.h = pg.enter_context(nc.semaphore("d_pg"))
            co.op("sync", lambda E_: E_.dma_start(out=dtd_sb[:], in_=DTD),
                  inc=(d_pg, 16), dma=True)
            co.op("sync", lambda E_: E_.dma_start(out=rjt_sb[:], in_=ROWJT8),
                  inc=(d_pg, 16), dma=True)
            co.op("sync", lambda E_: E_.dma_start(out=cjr_sb[:], in_=COLJR),
                  inc=(d_pg, 16), dma=True)
            pg_full = d_pg.n
            for oc in range(OC):
                co.op("vector", lambda E_, o=oc: E_.tensor_copy(
                    gcol[0:64, o:o + 1], gneg[0:64, 2 * o:2 * o + 1]),
                    inc=(sV, 1), waits=[(sV, v_gneg)])
                co.op("vector", lambda E_, o=oc: E_.tensor_copy(
                    gcol[64:128, o:o + 1], gneg[64:128, 2 * o + 1:2 * o + 2]),
                    inc=(sV, 1))
            v_gcol = sV.n
            for h in range(H):
                co.op("scalar", lambda E_, hh=h: E_.activation(
                    mdiag[:, hh * 128:(hh + 1) * 128], dtd_sb[:], AF.Exp,
                    scale=gneg[:, hh:hh + 1]),
                    inc=(sS, 1), waits=[(d_pg, pg_full), (sV, v_gneg)])
            co.op("vector", lambda E_: E_.tensor_copy(
                pstage[:].rearrange("p (a b) -> p a b", b=128),
                gcol[:].rearrange("p (a b) -> p a b", b=1)
                .broadcast_to([128, OC, 128])),
                inc=(sV, 1), waits=[(sV, v_gcol)])
            co.op("vector", lambda E_: E_.tensor_tensor(
                pstage[:], pstage[:], rjt_sb[:], ALU.mult),
                inc=(sV, 1), waits=[(d_pg, pg_full)])
            vfm = sV.n
            co.op("scalar", lambda E_: E_.activation(decfm[:], pstage[:], AF.Exp),
                  inc=(sS, 1), waits=[(sV, vfm)])
            s_decfm = sS.n
            co.op("vector", lambda E_: E_.tensor_copy(
                pstage[:].rearrange("p (a b) -> p a b", b=64),
                gneg[:].rearrange("p (a b) -> p a b", b=1)
                .broadcast_to([128, H, 64])),
                inc=(sV, 1), waits=[(sS, s_decfm)])
            co.op("vector", lambda E_: E_.tensor_scalar_mul(
                pstage[:], pstage[:], cjr_sb[:]), inc=(sV, 1))
            vbt = sV.n
            co.op("scalar", lambda E_: E_.activation(decBtm[:], pstage[:], AF.Exp),
                  inc=(sS, 1), waits=[(sV, vbt)])
            s_decb = sS.n
            co.op("vector", lambda E_: E_.tensor_copy(
                pstage[:].rearrange("p (a b) -> p a b", b=128),
                gcol[:].rearrange("p (a b) -> p a b", b=1)
                .broadcast_to([128, OC, 128])),
                inc=(sV, 1), waits=[(sS, s_decb)])
            vdc = sV.n
            co.op("scalar", lambda E_: E_.activation(
                dc128[:], pstage[:], AF.Exp, scale=128.0),
                inc=(sS, 1), waits=[(sV, vdc)])
            s_prolog = sS.n
            v_prolog = sV.n
            co.barrier([sS, sV])
            co.flush(blk)

        with ExitStack() as pa:
            WqT_sb = pa.enter_context(nc.sbuf_tensor("WqT_sb", [128, OC * E], BF16))
            WkT_sb = pa.enter_context(nc.sbuf_tensor("WkT_sb", [128, OC * E], BF16))
            WvT_sb = pa.enter_context(nc.sbuf_tensor("WvT_sb", [128, OC * E], BF16))
            WgT_sb = pa.enter_context(nc.sbuf_tensor("WgT_sb", [128, OC * E], BF16))
            bq_sb = pa.enter_context(nc.sbuf_tensor("bq_sb", [128, OC], F32))
            bk_sb = pa.enter_context(nc.sbuf_tensor("bk_sb", [128, OC], F32))
            bv_sb = pa.enter_context(nc.sbuf_tensor("bv_sb", [128, E], F32))
            bg_sb = pa.enter_context(nc.sbuf_tensor("bg_sb", [128, E], F32))
            cq_sb = pa.enter_context(nc.sbuf_tensor("cq_sb", [128, T], F32))
            sq_sb = pa.enter_context(nc.sbuf_tensor("sq_sb", [128, T], F32))
            ck_sb = pa.enter_context(nc.sbuf_tensor("ck_sb", [128, T], F32))
            sk_sb = pa.enter_context(nc.sbuf_tensor("sk_sb", [128, T], F32))
            xst = [pa.enter_context(nc.sbuf_tensor(f"xst{i}", [128, E], BF16))
                   for i in range(2)]
            xfm = pa.enter_context(nc.sbuf_tensor("xfm", [128, OC * TILE], BF16))
            qst = [pa.enter_context(nc.sbuf_tensor(f"qst{i}", [128, TILE], BF16))
                   for i in range(2)]
            tmpA = pa.enter_context(nc.sbuf_tensor("tmpA", [128, TILE], F32))
            tmpB = pa.enter_context(nc.sbuf_tensor("tmpB", [128, TILE], F32))
            qrst = [pa.enter_context(nc.sbuf_tensor(f"qrst{i}", [128, TILE], BF16))
                    for i in range(2)]
            vgpre = pa.enter_context(nc.sbuf_tensor("vgpre", [128, E], F32))
            vst = [pa.enter_context(nc.sbuf_tensor(f"vst{i}", [128, E], BF16))
                   for i in range(2)]
            gst = [pa.enter_context(nc.sbuf_tensor(f"gst{i}", [128, E], F32))
                   for i in range(2)]
            psT = [pa.enter_context(nc.psum_tensor(f"psT{i}", [128, 128], BF16))
                   for i in range(2)]
            pQK = [pa.enter_context(nc.psum_tensor(f"pQK{i}", [128, TILE], F32))
                   for i in range(2)]
            pR = pa.enter_context(nc.psum_tensor("pR", [128, TILE], F32))
            pVG = pa.enter_context(nc.psum_tensor("pVG", [128, E], F32))
            psT_tok = [(sV, 0), (sV, 0)]

            for src, dst in [(WqT, WqT_sb), (WkT, WkT_sb),
                             (WvT, WvT_sb), (WgT, WgT_sb)]:
                co.op("gpsimd", lambda E_, s_=src, d_=dst: E_.dma_start(
                    out=d_[:].rearrange("p (c f) -> p c f", c=OC),
                    in_=s_.rearrange("(c p) f -> p c f", p=128)),
                    inc=(d_wpg, 16), dma=True)
            for src, dst in [(bq2, bq_sb), (bk2, bk_sb), (bvb, bv_sb),
                             (bgb, bg_sb), (cosq, cq_sb), (sinq, sq_sb),
                             (cosk, ck_sb), (sink, sk_sb)]:
                co.op("sync", lambda E_, s_=src, d_=dst: E_.dma_start(
                    out=d_[:], in_=s_), inc=(d_wp, 16), dma=True)
            wp_full = d_wp.n
            wpg_full = d_wpg.n

            xst_rd = [0, 0]
            xst_ld = [0, 0]
            qst_tok = [(0, 0), (0, 0)]
            qr_dma = [0, 0]
            v_dma = [0, 0]
            g_dma = [0, 0]
            pQK_sv = [0, 0]
            pR_v = 0
            pVG_rd = 0
            pVG_rd_s = 0
            vgpre_rd_s = 0

            for gt in range(NTILE):
                s, ti = gt // TPS, gt % TPS
                t0 = ti * TILE
                for tc in range(4):
                    par = tc % 2
                    r0 = s * T + t0 + tc * 128
                    co.op("gpsimd", lambda E_, p=par, r=r0: E_.dma_start(
                        out=xst[p][:], in_=x[r:r + 128, :]),
                        inc=(d_x[par], 16), dma=True,
                        waits=[(sT, xst_rd[par])])
                    xst_ld[par] = d_x[par].n
                    for oc in range(OC):
                        tb = oc % 2
                        last = co.op("tensor", lambda E_, p=par, o=oc, t2=tb:
                                     E_.transpose(
                            psT[t2][:], xst[p][:, o * 128:(o + 1) * 128], id_sb[:]),
                            inc=(sT, 1),
                            waits=[(d_x[par], xst_ld[par]), psT_tok[tb],
                                   (d_const, 16 * 5)])
                        if oc == OC - 1:
                            xst_rd[par] = last
                        if oc % 4 < 2:
                            tok = co.op("vector",
                                        lambda E_, o=oc, c=tc, t2=tb: E_.tensor_copy(
                                xfm[:, o * TILE + c * 128: o * TILE + (c + 1) * 128],
                                psT[t2][:]), inc=(sV, 1), waits=[(sT, last)])
                            psT_tok[tb] = (sV, tok)
                        else:
                            tok = co.op("scalar",
                                        lambda E_, o=oc, c=tc, t2=tb: E_.activation(
                                xfm[:, o * TILE + c * 128: o * TILE + (c + 1) * 128],
                                psT[t2][:], AF.Copy), inc=(sS, 1), waits=[(sT, last)])
                            psT_tok[tb] = (sS, tok)
                v_xfm = sV.n
                s_xfm = sS.n
                co.wait("tensor", sS, s_xfm)

                for Wsb, bsb, ctab, stab, spill in [
                        (WqT_sb, bq_sb, cq_sb, sq_sb, Qs),
                        (WkT_sb, bk_sb, ck_sb, sk_sb, Ks)]:
                    for oc in range(OC):
                        rot = oc % 2
                        co.wait("tensor", d_wp, wp_full)
                        co.wait("tensor", d_wpg, wpg_full)
                        co.wait("tensor", sV, v_xfm)
                        co.wait("tensor", sS, pQK_sv[rot])
                        for fc in range(OC):
                            bld = (lambda E_, o=oc, f=fc, W=Wsb, r=rot:
                                   E_.matmul(pQK[r][:],
                                             W[:, f * E + o * 128:f * E + (o + 1) * 128],
                                             xfm[:, f * TILE:(f + 1) * TILE],
                                             start=(f == 0), stop=(f == OC - 1)))
                            if fc == OC - 1:
                                co.op("tensor", bld, inc=(sT, 1))
                            else:
                                co.raw("tensor", bld)
                        t_proj = sT.n
                        sevac = co.op("scalar", lambda E_, o=oc, r=rot, bb=bsb:
                                      E_.activation(qst[r][:], pQK[r][:], AF.Identity,
                                                    bias=bb[:, o:o + 1]),
                                      inc=(sS, 1),
                                      waits=[(sT, t_proj), (sT, qst_tok[rot][0]),
                                             (sV, qst_tok[rot][1]),
                                             (d_wp, wp_full), (d_wpg, wpg_full)])
                        pQK_sv[rot] = sevac
                        tperm = co.op("tensor", lambda E_, r=rot: E_.matmul(
                            pR[:], pm_sb[:], qst[r][:], start=True, stop=True),
                            inc=(sT, 1), waits=[(sS, sevac), (sV, pR_v)])
                        co.op("vector", lambda E_, r=rot, ct=ctab, tz=t0: E_.tensor_tensor(
                            tmpA[:], qst[r][:], ct[:, tz:tz + TILE], ALU.mult),
                            inc=(sV, 1), waits=[(sS, sevac), (d_wp, wp_full), (d_wpg, wpg_full)])
                        vrope1 = co.op("vector", lambda E_, st_=stab, tz=t0: E_.tensor_tensor(
                            tmpB[:], pR[:], st_[:, tz:tz + TILE], ALU.mult),
                            inc=(sV, 1), waits=[(sT, tperm)])
                        pR_v = vrope1
                        vrope2 = co.op("vector", lambda E_, r=rot: E_.tensor_tensor(
                            qrst[r][:], tmpA[:], tmpB[:], ALU.add),
                            inc=(sV, 1),
                            waits=[(d_spq[rot], qr_dma[rot])])
                        qst_tok[rot] = (tperm, vrope1)
                        co.op("sync", lambda E_, r=rot, o=oc, sp=spill, ss=s, tz=t0:
                              E_.dma_start(out=sp[ss, o, :, tz:tz + TILE],
                                           in_=qrst[r][:]),
                              inc=(d_spq[rot], 16), dma=True, waits=[(sV, vrope2)])
                        qr_dma[rot] = d_spq[rot].n

                for tc in range(4):
                    kcg = ti * 4 + tc
                    for which in ("v", "g"):
                        Wsb = WvT_sb if which == "v" else WgT_sb
                        co.wait("tensor", sV, pVG_rd)
                        co.wait("tensor", sS, pVG_rd_s)
                        for half in range(2):
                            for fc in range(OC):
                                is_last = (half == 1 and fc == OC - 1)
                                bld = (lambda E_, f=fc, h2=half, c=tc, W=Wsb:
                                       E_.matmul(pVG[:, h2 * 512:(h2 + 1) * 512],
                                                 xfm[:, f * TILE + c * 128:
                                                     f * TILE + (c + 1) * 128],
                                                 W[:, f * E + h2 * 512:f * E + (h2 + 1) * 512],
                                                 start=(f == 0), stop=(f == OC - 1)))
                                if is_last:
                                    co.op("tensor", bld, inc=(sT, 1))
                                else:
                                    co.raw("tensor", bld)
                        t_vg = sT.n
                        if which == "v":
                            par = tc % 2
                            vv = co.op("vector", lambda E_, p=par: E_.tensor_tensor(
                                vst[p][:], pVG[:], bv_sb[:], ALU.add),
                                inc=(sV, 1),
                                waits=[(sT, t_vg), (d_spv[par], v_dma[par]),
                                       (d_wp, wp_full), (d_wpg, wpg_full)])
                            pVG_rd = vv
                            co.op("sync", lambda E_, p=par, ss=s, k=kcg: E_.dma_start(
                                out=Vs[ss, k], in_=vst[p][:]),
                                inc=(d_spv[par], 16), dma=True, waits=[(sV, vv)])
                            v_dma[par] = d_spv[par].n
                        else:
                            par = tc % 2
                            vg = co.op("vector", lambda E_: E_.tensor_tensor(
                                vgpre[:], pVG[:], bg_sb[:], ALU.add),
                                inc=(sV, 1),
                                waits=[(sT, t_vg), (sS, vgpre_rd_s),
                                       (d_wp, wp_full), (d_wpg, wpg_full)])
                            pVG_rd = vg
                            sg = co.op("scalar", lambda E_, p=par: E_.activation(
                                gst[p][:], vgpre[:], AF.Silu),
                                inc=(sS, 1),
                                waits=[(sV, vg), (d_spg[par], g_dma[par])])
                            vgpre_rd_s = sg
                            pVG_rd_s = sg
                            co.op("sync", lambda E_, p=par, ss=s, k=kcg: E_.dma_start(
                                out=Gs[ss, k], in_=gst[p][:]),
                                inc=(d_spg[par], 16), dma=True, waits=[(sS, sg)])
                            g_dma[par] = d_spg[par].n

            co.barrier([sT, sV, sS] + d_spq + d_spv + d_spg)
            co.flush(blk)

        with ExitStack() as pb:
            Q_sb = pb.enter_context(nc.sbuf_tensor("Q_sb", [128, OC * T], BF16))
            K_sb = pb.enter_context(nc.sbuf_tensor("K_sb", [128, OC * T], BF16))
            V_sb = pb.enter_context(nc.sbuf_tensor("V_sb", [128, KC * E], BF16))
            O_sb = pb.enter_context(nc.sbuf_tensor("O_sb", [128, OC * T], BF16))

            for s in range(SEQ_PER_CORE):
                co.op("scalar", lambda E_, ss=s: E_.dma_start(
                    out=Q_sb[:].rearrange("p (o t) -> p o t", o=OC),
                    in_=Qs[ss].rearrange("o p t -> p o t")),
                    inc=(d_qkv, 16), dma=True, waits=d_spq)
                co.op("scalar", lambda E_, ss=s: E_.dma_start(
                    out=K_sb[:].rearrange("p (o t) -> p o t", o=OC),
                    in_=Ks[ss].rearrange("o p t -> p o t")),
                    inc=(d_qkv, 16), dma=True, waits=d_spq)
                co.op("scalar", lambda E_, ss=s: E_.dma_start(
                    out=V_sb[:].rearrange("p (k e) -> p k e", k=KC),
                    in_=Vs[ss].rearrange("k p e -> p k e")),
                    inc=(d_qkv, 16), dma=True, waits=d_spv)
                qkv_full = d_qkv.n

                with ExitStack() as pbp:
                    KT_sb = pbp.enter_context(
                        nc.sbuf_tensor(f"KT_sb{s}", [128, KC * E], BF16))
                    F_bf = pbp.enter_context(
                        nc.sbuf_tensor(f"F_bf{s}", [128, KC * 512], BF16))
                    Qhat = pbp.enter_context(
                        nc.sbuf_tensor(f"Qhat{s}", [128, E], BF16))
                    Khat = pbp.enter_context(
                        nc.sbuf_tensor(f"Khat{s}", [128, E], BF16))
                    Rall = pbp.enter_context(
                        nc.sbuf_tensor(f"Rall{s}", [128, E], F32))
                    Rbf = pbp.enter_context(
                        nc.sbuf_tensor(f"Rbf{s}", [128, E], BF16))
                    PTs = [pbp.enter_context(
                        nc.sbuf_tensor(f"PTs{s}_{i}", [128, 128], BF16))
                        for i in range(4)]

                    with ExitStack() as ptr:
                        psT3 = [ptr.enter_context(
                            nc.psum_tensor(f"psT3{s}_{i}", [128, 128], BF16))
                            for i in range(2)]
                        psT3_tok = [(sV, 0), (sV, 0)]
                        for tc in range(KC):
                            for oc in range(OC):
                                i2_ = (tc * OC + oc) % 2
                                pse, psv = psT3_tok[i2_]
                                tt = co.op("tensor", lambda E_, o=oc, k=tc, i3=i2_:
                                           E_.transpose(
                                               psT3[i3][:],
                                               K_sb[:, o * T + k * 128:o * T + (k + 1) * 128],
                                               id_sb[:]),
                                           inc=(sT, 1),
                                           waits=[(d_qkv, qkv_full), (pse, psv)])
                                if (tc * OC + oc) % 4 < 2:
                                    tok = co.op(
                                        "vector", lambda E_, o=oc, k=tc, i3=i2_:
                                        E_.tensor_copy(
                                            KT_sb[:, k * E + o * 128:k * E + (o + 1) * 128],
                                            psT3[i3][:]),
                                        inc=(sV, 1), waits=[(sT, tt)])
                                    psT3_tok[i2_] = (sV, tok)
                                else:
                                    tok = co.op(
                                        "scalar", lambda E_, o=oc, k=tc, i3=i2_:
                                        E_.activation(
                                            KT_sb[:, k * E + o * 128:k * E + (o + 1) * 128],
                                            psT3[i3][:], AF.Copy),
                                        inc=(sS, 1), waits=[(sT, tt)])
                                    psT3_tok[i2_] = (sS, tok)
                        v_kt = sV.n
                        s_kt = sS.n

                    with ExitStack() as pfs:
                        psumF = [pfs.enter_context(
                            nc.psum_tensor(f"psumF{s}_{i}", [128, 128], F32))
                            for i in range(4)]
                        fe_tok = [[] for _ in range(4)]
                        for oc in range(OC):
                            fb = oc % 4
                            for J in range(KC - 1, 0, -1):
                                co.wait("tensor", sV, v_kt)
                                co.wait("tensor", sS, s_kt)
                                for fsem, fcnt in fe_tok[fb]:
                                    co.wait("tensor", fsem, fcnt)
                                tj = co.op("tensor", lambda E_, o=oc, JJ=J, f2=fb:
                                           E_.matmul(
                                    psumF[f2][:],
                                    KT_sb[:, JJ * E + o * 128:JJ * E + (o + 1) * 128],
                                    V_sb[:, JJ * E + o * 128:JJ * E + (o + 1) * 128],
                                    start=(JJ == KC - 1), stop=(JJ == 1),
                                    skip_group_check=True),
                                    inc=(sT, 1))
                                s2_ = co.op("scalar", lambda E_, o=oc, JJ=J, f2=fb:
                                      E_.activation(
                                          F_bf[0:64, (JJ - 1) * 512 + o * 64:
                                               (JJ - 1) * 512 + o * 64 + 64],
                                          psumF[f2][0:64, 0:64], AF.Copy),
                                      inc=(sS, 1), waits=[(sT, tj)])
                                v2 = co.op(
                                    "vector", lambda E_, o=oc, JJ=J, f2=fb:
                                    E_.tensor_copy(
                                        F_bf[64:128, (JJ - 1) * 512 + o * 64:
                                             (JJ - 1) * 512 + o * 64 + 64],
                                        psumF[f2][64:128, 64:128]),
                                    inc=(sV, 1), waits=[(sT, tj)])
                                fe_tok[fb] = [(sS, s2_), (sV, v2)]
                    s_fdone = sS.n
                    vf_done = sV.n
                    co.wait("tensor", sS, s_fdone)
                    co.wait("tensor", sV, vf_done)

                    with ExitStack() as pfw:
                        pSTd = [pfw.enter_context(
                            nc.psum_tensor(f"pSTd{s}_{i}", [128, 128], F32))
                            for i in range(2)]
                        pOutE = [pfw.enter_context(
                            nc.psum_tensor(f"pOutE{s}_{i}", [128, 128], F32))
                            for i in range(2)]
                        pOutO = [pfw.enter_context(
                            nc.psum_tensor(f"pOutO{s}_{i}", [128, 128], F32))
                            for i in range(2)]
                        psumS = [pfw.enter_context(
                            nc.psum_tensor(f"psumS{s}_{i}", [128, 128], F32))
                            for i in range(2)]

                        co.wait("tensor", sS, s_fdone)
                        qhat_rd = 0
                        khat_rd = 0
                        rbf_rd = 0
                        radd_v = [0, 0]
                        pt_rd = [0, 0, 0, 0]
                        pstd_rd = [0, 0]
                        poute_rd = [0, 0]
                        pouto_rd = [0, 0]
                        for I in range(KC):
                            vq = co.op("vector", lambda E_, II=I: E_.tensor_tensor(
                                Qhat[:].rearrange("p (o j) -> p o j", o=OC),
                                Q_sb[:].rearrange("p (o t) -> p o t", o=OC)
                                [:, :, II * 128:(II + 1) * 128],
                                decfm[:].rearrange("p (o j) -> p o j", o=OC),
                                ALU.mult),
                                inc=(sV, 1),
                                waits=[(sT, qhat_rd), (sS, s_decfm),
                                       (d_qkv, qkv_full)])
                            vk = co.op("vector", lambda E_, II=I: E_.tensor_tensor(
                                Khat[:], KT_sb[:, II * E:(II + 1) * E], decBtm[:],
                                ALU.mult),
                                inc=(sV, 1), waits=[(sT, khat_rd), (sS, s_decb),
                                                    (sV, v_kt), (sS, s_kt)])
                            if I > 0:
                                vrbf = co.op("vector", lambda E_: E_.tensor_copy(
                                    Rbf[:], Rall[:]), inc=(sV, 1),
                                    waits=[(sT, rbf_rd)])
                                co.op("vector", lambda E_: E_.tensor_tensor(
                                    Rall[:], Rall[:], dc128[:], ALU.mult),
                                    inc=(sV, 1), waits=[(sS, s_prolog)])
                            else:
                                vrbf = 0
                            for oc in range(OC):
                                sb2 = oc % 2
                                co.wait("tensor", sV, vk)
                                co.wait("tensor", sV, radd_v[sb2])
                                ts2 = co.op("tensor", lambda E_, o=oc, II=I, b3=sb2:
                                            E_.matmul(
                                    psumS[b3][:],
                                    Khat[:, o * 128:(o + 1) * 128],
                                    V_sb[:, II * E + o * 128:II * E + (o + 1) * 128],
                                    start=True, stop=True,
                                    skip_group_check=True),
                                    inc=(sT, 1))
                                if I == 0:
                                    radd_v[sb2] = co.op(
                                        "vector", lambda E_, o=oc, b3=sb2:
                                        E_.tensor_copy(
                                            Rall[:, o * 128:(o + 1) * 128],
                                            psumS[b3][:]),
                                        inc=(sV, 1), waits=[(sT, ts2)])
                                else:
                                    radd_v[sb2] = co.op(
                                        "vector", lambda E_, o=oc, b3=sb2:
                                        E_.tensor_tensor(
                                            Rall[:, o * 128:(o + 1) * 128],
                                            Rall[:, o * 128:(o + 1) * 128],
                                            psumS[b3][:], ALU.add),
                                        inc=(sV, 1), waits=[(sT, ts2)])
                            for oc in range(OC):
                                h0, h1 = 2 * oc, 2 * oc + 1
                                pe, po = (2 * oc) % 4, (2 * oc + 1) % 4
                                r5 = oc % 2
                                tse = co.op("tensor", lambda E_, o=oc, II=I:
                                            E_.matmul(
                                    pSTd[0][:],
                                    K_sb[0:64, o * T + II * 128:o * T + (II + 1) * 128],
                                    Q_sb[0:64, o * T + II * 128:o * T + (II + 1) * 128],
                                    start=True, stop=True, skip_group_check=True),
                                    inc=(sT, 1), waits=[(sV, pstd_rd[0])])
                                tso = co.op("tensor", lambda E_, o=oc, II=I:
                                            E_.matmul(
                                    pSTd[1][:],
                                    K_sb[64:128, o * T + II * 128:o * T + (II + 1) * 128],
                                    Q_sb[64:128, o * T + II * 128:o * T + (II + 1) * 128],
                                    start=True, stop=True, skip_group_check=True),
                                    inc=(sT, 1), waits=[(sV, pstd_rd[1])])
                                vpe = co.op("vector", lambda E_, hh=h0, p4=pe:
                                            E_.scalar_tensor_tensor(
                                    PTs[p4][:], mdiag[:, hh * 128:(hh + 1) * 128],
                                    1.0, pSTd[0][:], ALU.min, ALU.mult),
                                    inc=(sV, 1),
                                    waits=[(sT, tse), (sT, pt_rd[pe])])
                                pstd_rd[0] = vpe
                                vpo = co.op("vector", lambda E_, hh=h1, p4=po:
                                            E_.scalar_tensor_tensor(
                                    PTs[p4][:], mdiag[:, hh * 128:(hh + 1) * 128],
                                    1.0, pSTd[1][:], ALU.min, ALU.mult),
                                    inc=(sV, 1),
                                    waits=[(sT, tso), (sT, pt_rd[po])])
                                pstd_rd[1] = vpo
                                for par2, hh, p4, ptile, prd in (
                                        (0, h0, pe, pOutE, poute_rd),
                                        (64, h1, po, pOutO, pouto_rd)):
                                    co.wait("tensor", sS, prd[r5])
                                    co.wait("tensor", sV, vpo if par2 else vpe)
                                    mm_list = []
                                    mm_list.append(
                                        lambda E_, b4=par2, h5=hh, p5=p4, r6=r5,
                                        II=I, pt2=ptile: E_.matmul(
                                            pt2[r6][b4:b4 + 64, :],
                                            V_sb[:, II * E + h5 * 64:II * E + h5 * 64 + 64],
                                            PTs[p5][:], start=True, stop=False,
                                            tile_position=(0, b4),
                                            skip_group_check=True))
                                    if I > 0:
                                        co.wait("tensor", sV, vrbf)
                                        mm_list.append(
                                            lambda E_, b4=par2, o=oc, r6=r5,
                                            II=I, pt2=ptile: E_.matmul(
                                                pt2[r6][b4:b4 + 64, :],
                                                Rbf[b4:b4 + 64,
                                                    o * 128 + b4:o * 128 + b4 + 64],
                                                Qhat[b4:b4 + 64,
                                                     o * 128:(o + 1) * 128],
                                                start=False, stop=False,
                                                tile_position=(b4, b4),
                                                skip_group_check=True))
                                        co.wait("tensor", sV, vq)
                                    if I < KC - 1:
                                        co.wait("tensor", sS, s_fdone)
                                        mm_list.append(
                                            lambda E_, b4=par2, o=oc, r6=r5,
                                            II=I, pt2=ptile: E_.matmul(
                                                pt2[r6][b4:b4 + 64, :],
                                                F_bf[b4:b4 + 64,
                                                     II * 512 + o * 64:II * 512 + o * 64 + 64],
                                                Q_sb[b4:b4 + 64,
                                                     o * T + II * 128:o * T + (II + 1) * 128],
                                                start=False, stop=False,
                                                tile_position=(b4, b4),
                                                skip_group_check=True))
                                    for mi, bld in enumerate(mm_list):
                                        if mi == len(mm_list) - 1:
                                            co.op("tensor", bld, inc=(sT, 1))
                                        else:
                                            co.raw("tensor", bld)
                                    t_out = sT.n
                                    pt_rd[p4] = t_out
                                    if I > 0:
                                        rbf_rd = t_out
                                    qhat_rd = t_out
                                    prd[r5] = co.op(
                                        "scalar", lambda E_, b4=par2, o=oc, II=I,
                                        r6=r5, pt2=ptile: E_.activation(
                                            O_sb[b4:b4 + 64,
                                                 o * T + II * 128:o * T + (II + 1) * 128],
                                            pt2[r6][b4:b4 + 64, :], AF.Copy),
                                        inc=(sS, 1), waits=[(sT, t_out)])
                            khat_rd = sT.n
                    v_heads_done = sV.n
                    s_heads_done = sS.n
                    co.barrier([sT, sV, sS])
                    co.flush(blk)

                with ExitStack() as pc:
                    Gld = [pc.enter_context(nc.sbuf_tensor(f"Gld{s}_{i}", [128, E], F32))
                           for i in range(2)]
                    Xld = [pc.enter_context(nc.sbuf_tensor(f"Xld{s}_{i}", [128, E], F32))
                           for i in range(2)]
                    Otm = pc.enter_context(nc.sbuf_tensor(f"Otm{s}", [128, E], F32))
                    Rr = pc.enter_context(nc.sbuf_tensor(f"Rr{s}", [128, E], F32))
                    sqb = pc.enter_context(nc.sbuf_tensor(f"sqb{s}", [128, E], F32))
                    X1tm = [pc.enter_context(nc.sbuf_tensor(f"X1tm{s}_{i}", [128, E], F32))
                            for i in range(2)]
                    X1bfs = pc.enter_context(nc.sbuf_tensor(f"X1bfs{s}", [128, E], BF16))
                    Xfms = [pc.enter_context(
                        nc.sbuf_tensor(f"Xfms{s}_{i}", [128, OC * TILE], BF16))
                        for i in range(2)]
                    stat = pc.enter_context(nc.sbuf_tensor(f"stat{s}", [128, 8], F32))
                    logit = pc.enter_context(nc.sbuf_tensor(f"logit{s}", [128, NE], F32))
                    mk1 = pc.enter_context(nc.sbuf_tensor(f"mk1{s}", [128, NE], F32))
                    mbf = pc.enter_context(nc.sbuf_tensor(f"mbf{s}", [128, NE], BF16))
                    mT_sb = pc.enter_context(nc.sbuf_tensor(f"mT_sb{s}", [8, 128], BF16))
                    Yst = pc.enter_context(nc.sbuf_tensor(f"Yst{s}", [128, E], F32))
                    psT2 = [pc.enter_context(
                        nc.psum_tensor(f"psT2{s}_{i}", [128, 128], BF16))
                        for i in range(2)]
                    pL = pc.enter_context(nc.psum_tensor(f"pL{s}", [128, NE], F32))
                    pMT = pc.enter_context(nc.psum_tensor(f"pMT{s}", [8, 128], BF16))
                    pSel = [pc.enter_context(
                        nc.psum_tensor(f"pSel{s}_{i}", [128, 512], F32)) for i in range(2)]

                    g_ld = [0, 0]
                    x_ld = [0, 0]
                    g_rd = [0, 0]
                    x_rd = [0, 0]
                    psT2_tok = [(sV, 0), (sV, 0)]
                    x1tm_rd = [0, 0]
                    x1tm_dma = [0, 0]
                    xfms_dma = [0, 0]
                    xfms_rd = [0, 0]
                    ys_dma = 0

                    for tc in range(KC):
                        par = tc % 2
                        gt = s * TPS + tc // 4
                        tcw = tc % 4
                        fpar = (tc // 4) % 2
                        r0 = s * T + tc * 128
                        co.op("scalar", lambda E_, p=par, ss=s, k=tc: E_.dma_start(
                            out=Gld[p][:], in_=Gs[ss, k]),
                            inc=(d_g[par], 16), dma=True,
                            waits=[(sV, g_rd[par])] + d_spg)
                        g_ld[par] = d_g[par].n
                        co.op("sync", lambda E_, p=par, r=r0: E_.dma_start(
                            out=Xld[p][:], in_=x[r:r + 128, :]),
                            inc=(d_xf[par], 16), dma=True, waits=[(sV, x_rd[par])])
                        x_ld[par] = d_xf[par].n
                        for oc in range(OC):
                            tb = oc % 2
                            tt = co.op("tensor", lambda E_, o=oc, k=tc, t2=tb:
                                       E_.transpose(
                                psT2[t2][:], O_sb[:, o * T + k * 128:o * T + (k + 1) * 128],
                                id_sb[:]),
                                inc=(sT, 1), waits=[psT2_tok[tb], (sV, v_heads_done),
                                                    (sS, s_heads_done)])
                            if oc % 4 < 2:
                                tok = co.op("vector",
                                            lambda E_, o=oc, t2=tb: E_.tensor_copy(
                                    Otm[:, o * 128:(o + 1) * 128], psT2[t2][:]),
                                    inc=(sV, 1), waits=[(sT, tt)])
                                psT2_tok[tb] = (sV, tok)
                            else:
                                tok = co.op("scalar",
                                            lambda E_, o=oc, t2=tb: E_.activation(
                                    Otm[:, o * 128:(o + 1) * 128], psT2[t2][:],
                                    AF.Copy),
                                    inc=(sS, 1), waits=[(sT, tt)])
                                psT2_tok[tb] = (sS, tok)
                        co.wait("vector", sS, sS.n)
                        co.op("vector", lambda E_, p=par: E_.tensor_tensor(
                            Otm[:], Otm[:], Gld[p][:], ALU.mult),
                            inc=(sV, 1), waits=[(d_g[par], g_ld[par])])
                        g_rd[par] = sV.n
                        co.op("vector", lambda E_, p=par: E_.tensor_tensor(
                            Rr[:], Otm[:], Xld[p][:], ALU.add),
                            inc=(sV, 1), waits=[(d_xf[par], x_ld[par])])
                        x_rd[par] = sV.n
                        co.op("vector", lambda E_: E_.reduce_sum(
                            stat[:, 0:1], Rr[:], axis=AX.X), inc=(sV, 1))
                        co.op("vector", lambda E_: E_.tensor_scalar_mul(
                            stat[:, 1:2], stat[:, 0:1], 1.0 / E), inc=(sV, 1))
                        ssq = co.op("scalar", lambda E_: E_.activation(
                            sqb[:], Rr[:], AF.Square, accum_out=stat[:, 2:3]),
                            inc=(sS, 1), waits=[(sV, x_rd[par])])
                        co.wait("vector", sS, ssq)
                        co.op("vector", lambda E_: E_.tensor_tensor(
                            stat[:, 3:4], stat[:, 1:2], stat[:, 1:2], ALU.mult),
                            inc=(sV, 1))
                        co.op("vector", lambda E_: E_.scalar_tensor_tensor(
                            stat[:, 4:5], stat[:, 2:3], 1.0 / E, stat[:, 3:4],
                            ALU.mult, ALU.subtract), inc=(sV, 1))
                        vvar = co.op("vector", lambda E_: E_.tensor_scalar_add(
                            stat[:, 5:6], stat[:, 4:5], 1e-5), inc=(sV, 1))
                        co.op("scalar", lambda E_: E_.activation(
                            stat[:, 6:7], stat[:, 5:6], AF.Ln), inc=(sS, 1),
                            waits=[(sV, vvar)])
                        ssd = co.op("scalar", lambda E_: E_.activation(
                            stat[:, 7:8], stat[:, 6:7], AF.Exp, scale=-0.5),
                            inc=(sS, 1))
                        co.wait("vector", sS, ssd)
                        vx1 = co.op("vector", lambda E_, p=par: E_.tensor_scalar(
                            X1tm[p][:], Rr[:], stat[:, 1:2], stat[:, 7:8],
                            ALU.subtract, ALU.mult), inc=(sV, 1),
                            waits=[(d_x1f[par], x1tm_dma[par]), (sV, x1tm_rd[par])])
                        co.op("sync", lambda E_, p=par, r=r0 // 128: E_.dma_start(
                            out=X1f[r], in_=X1tm[p][:]),
                            inc=(d_x1f[par], 16), dma=True, waits=[(sV, vx1)])
                        x1tm_dma[par] = d_x1f[par].n
                        vbf = co.op("vector", lambda E_, p=par: E_.tensor_copy(
                            X1bfs[:], X1tm[p][:]), inc=(sV, 1))
                        for oc in range(OC):
                            tb = oc % 2
                            tt = co.op("tensor", lambda E_, o=oc, t2=tb: E_.transpose(
                                psT2[t2][:], X1bfs[:, o * 128:(o + 1) * 128], id_sb[:]),
                                inc=(sT, 1),
                                waits=[(sV, vbf), psT2_tok[tb],
                                       (d_x1b[fpar], xfms_dma[fpar]),
                                       (sT, xfms_rd[fpar])])
                            if oc % 4 < 2:
                                tok = co.op("vector",
                                            lambda E_, o=oc, w=tcw, f=fpar, t2=tb:
                                            E_.tensor_copy(
                                                Xfms[f][:, o * TILE + w * 128:
                                                        o * TILE + (w + 1) * 128],
                                                psT2[t2][:]),
                                            inc=(sV, 1), waits=[(sT, tt)])
                                psT2_tok[tb] = (sV, tok)
                            else:
                                tok = co.op("scalar",
                                            lambda E_, o=oc, w=tcw, f=fpar, t2=tb:
                                            E_.activation(
                                                Xfms[f][:, o * TILE + w * 128:
                                                        o * TILE + (w + 1) * 128],
                                                psT2[t2][:], AF.Copy),
                                            inc=(sS, 1), waits=[(sT, tt)])
                                psT2_tok[tb] = (sS, tok)
                        v_xfms = sV.n
                        s_xfms = sS.n
                        co.wait("tensor", sV, v_xfms)
                        co.wait("tensor", sS, s_xfms)
                        co.wait("tensor", d_const)
                        co.wait("tensor", d_constg)
                        for oc in range(OC):
                            bld = (lambda E_, o=oc, w=tcw, f=fpar: E_.matmul(
                                pL[:],
                                Xfms[f][:, o * TILE + w * 128:o * TILE + (w + 1) * 128],
                                Wgf_sb[:, o * NE:(o + 1) * NE],
                                start=(o == 0), stop=(o == OC - 1)))
                            if oc == OC - 1:
                                co.op("tensor", bld, inc=(sT, 1))
                            else:
                                co.raw("tensor", bld)
                        t_logit = sT.n
                        co.op("vector", lambda E_: E_.tensor_tensor(
                            logit[:], pL[:], bgf_sb[:], ALU.add),
                            inc=(sV, 1), waits=[(sT, t_logit), d_const])
                        vnm = co.op("vector", lambda E_: E_.reduce_max(
                            stat[:, 0:1], logit[:], axis=AX.X, negate=True),
                            inc=(sV, 1))
                        sex = co.op("scalar", lambda E_: E_.activation(
                            mk1[:], logit[:], AF.Exp, bias=stat[:, 0:1]),
                            inc=(sS, 1), waits=[(sV, vnm)])
                        co.op("vector", lambda E_: E_.reduce_sum(
                            stat[:, 1:2], mk1[:], axis=AX.X),
                            inc=(sV, 1), waits=[(sS, sex)])
                        co.op("vector", lambda E_: E_.reciprocal(
                            stat[:, 2:3], stat[:, 1:2]), inc=(sV, 1))
                        co.op("vector", lambda E_: E_.tensor_scalar(
                            mk1[:], logit[:], stat[:, 0:1], 0.0, ALU.add, ALU.is_ge),
                            inc=(sV, 1))
                        vmw = co.op("vector", lambda E_, g=gt, w=tcw: E_.tensor_scalar(
                            maskW[:, (g * 4 + w) * NE:(g * 4 + w + 1) * NE],
                            mk1[:], stat[:, 2:3], None, ALU.mult),
                            inc=(sV, 1))
                        vmb = co.op("vector", lambda E_, g=gt, w=tcw: E_.tensor_copy(
                            mbf[:], maskW[:, (g * 4 + w) * NE:(g * 4 + w + 1) * NE]),
                            inc=(sV, 1))
                        tmt = co.op("tensor", lambda E_: E_.transpose(
                            pMT[:], mbf[:], id_sb[:]),
                            inc=(sT, 1), waits=[(sV, vmb)])
                        vmt = co.op("vector", lambda E_: E_.tensor_copy(
                            mT_sb[:], pMT[:]), inc=(sV, 1), waits=[(sT, tmt)])
                        for half in range(2):
                            co.op("tensor", lambda E_, hh=half: E_.matmul(
                                pSel[hh][:], mT_sb[:],
                                b2bf[:, hh * 512:(hh + 1) * 512],
                                start=True, stop=True),
                                inc=(sT, 1), waits=[(sV, vmt), d_const, d_constg])
                        t_sel = sT.n
                        for half in range(2):
                            co.op("vector", lambda E_, hh=half, p=par:
                                  E_.tensor_tensor(
                                      Yst[:, hh * 512:(hh + 1) * 512], pSel[hh][:],
                                      X1tm[p][:, hh * 512:(hh + 1) * 512], ALU.add),
                                  inc=(sV, 1),
                                  waits=[(sT, t_sel), (d_ysi, ys_dma)])
                        vys = sV.n
                        x1tm_rd[par] = vys
                        co.op("sync", lambda E_, r=r0 // 128: E_.dma_start(
                            out=Ys[r], in_=Yst[:]),
                            inc=(d_ysi, 16), dma=True, waits=[(sV, vys)])
                        ys_dma = d_ysi.n
                        if tcw == 3:
                            co.op("sync", lambda E_, g=gt, f=fpar: E_.dma_start(
                                out=X1bf[g].rearrange("o p t -> p o t"),
                                in_=Xfms[f][:].rearrange("p (o t) -> p o t", o=OC)),
                                inc=(d_x1b[fpar], 16), dma=True, waits=[(sV, v_xfms), (sS, s_xfms)])
                            xfms_dma[fpar] = d_x1b[fpar].n
                            xfms_rd[fpar] = sT.n
                    co.barrier([sT, sV, sS, d_ysi] + d_x1f + d_x1b)
                    co.flush(blk)

        with ExitStack() as pd_:
            W1h = [pd_.enter_context(
                nc.sbuf_tensor(f"W1h{i}", [128, OC * (FH // 2)], BF16))
                for i in range(2)]
            W2h = [pd_.enter_context(
                nc.sbuf_tensor(f"W2h{i}", [128, HC_HALF * E], BF16))
                for i in range(2)]
            X1t = [pd_.enter_context(
                nc.sbuf_tensor(f"X1t{i}", [128, OC * TILE], BF16))
                for i in range(2)]
            H_sb = pd_.enter_context(
                nc.sbuf_tensor("H_sb", [128, HC_HALF * TILE], BF16))
            Ost = [pd_.enter_context(nc.sbuf_tensor(f"Ost{i}", [128, E], F32))
                   for i in range(3)]
            pH = [pd_.enter_context(nc.psum_tensor(f"pH{i}", [128, TILE], F32))
                  for i in range(2)]
            pO = [pd_.enter_context(nc.psum_tensor(f"pO{i}", [128, 512], F32))
                  for i in range(4)]

            w1_ld = [0, 0]
            w2_ld = [0, 0]
            x1_ld = [0, 0]
            x1_rd = [0, 0]
            pH_rd = [0, 0]
            pO_rd = [0, 0, 0, 0]
            ost_dma = [0, 0, 0]
            ost_cyc = 0

            def load_whalf(e, half):
                par = (e * 2 + half) % 2
                cur = sT.n
                co.op("gpsimd", lambda E_, ee=e, hh=half, p=par: E_.dma_start(
                    out=W1h[p][:].rearrange("p (c f) -> p c f", c=OC),
                    in_=W1[ee, :, hh * (FH // 2):(hh + 1) * (FH // 2)]
                    .rearrange("(c p) f -> p c f", p=128)),
                    inc=(d_w1[par], 16), dma=True, waits=[(sT, cur)])
                w1_ld[par] = d_w1[par].n
                co.op("gpsimd", lambda E_, ee=e, hh=half, p=par: E_.dma_start(
                    out=W2h[p][:].rearrange("p (c f) -> p c f", c=HC_HALF),
                    in_=W2[ee, hh * (FH // 2):(hh + 1) * (FH // 2), :]
                    .rearrange("(c p) f -> p c f", p=128)),
                    inc=(d_w2[par], 16), dma=True, waits=[(sT, cur)])
                w2_ld[par] = d_w2[par].n

            load_whalf(0, 0)
            load_whalf(0, 1)
            for e in range(NE):
                for half in range(2):
                    par = (e * 2 + half) % 2
                    for gt in range(NTILE):
                        xpar = gt % 2
                        co.op("scalar", lambda E_, g=gt, p=xpar: E_.dma_start(
                            out=X1t[p][:].rearrange("p (o t) -> p o t", o=OC),
                            in_=X1bf[g].rearrange("o p t -> p o t")),
                            inc=(d_x1t[xpar], 16), dma=True,
                            waits=[(sT, x1_rd[xpar])] + d_x1b)
                        x1_ld[xpar] = d_x1t[xpar].n
                        co.wait("tensor", d_w1[par], w1_ld[par])
                        co.wait("tensor", d_x1t[xpar], x1_ld[xpar])
                        for hc in range(HC_HALF):
                            hrot = hc % 2
                            co.wait("tensor", sS, pH_rd[hrot])
                            for fc in range(OC):
                                bld = (lambda E_, f=fc, c=hc, p=par, xp=xpar:
                                       E_.matmul(
                                           pH[c % 2][:],
                                           W1h[p][:, f * (FH // 2) + c * 128:
                                                  f * (FH // 2) + (c + 1) * 128],
                                           X1t[xp][:, f * TILE:(f + 1) * TILE],
                                           start=(f == 0), stop=(f == OC - 1)))
                                if fc == OC - 1:
                                    co.op("tensor", bld, inc=(sT, 1))
                                else:
                                    co.raw("tensor", bld)
                            t_h = sT.n
                            pH_rd[hrot] = co.op(
                                "scalar", lambda E_, c=hc, ee=e, hh=half:
                                E_.activation(
                                    H_sb[:, c * TILE:(c + 1) * TILE], pH[c % 2][:],
                                    AF.Silu,
                                    bias=b1_sb[:, ee * 32 + hh * 16 + c:
                                               ee * 32 + hh * 16 + c + 1]),
                                inc=(sS, 1), waits=[(sT, t_h), d_const])
                        s_h_done = sS.n
                        x1_rd[xpar] = sT.n
                        co.wait("tensor", sS, s_h_done)
                        co.wait("tensor", d_w2[par], w2_ld[par])
                        for w in range(4):
                            ocy = ost_cyc % 3
                            for foh in range(2):
                                ob = (w * 2 + foh) % 4
                                co.wait("tensor", sV, pO_rd[ob])
                                for hc in range(HC_HALF):
                                    bld = (lambda E_, c=hc, ww=w, f=foh, p=par, b_=ob:
                                           E_.matmul(
                                               pO[b_][:],
                                               H_sb[:, c * TILE + ww * 128:
                                                    c * TILE + (ww + 1) * 128],
                                               W2h[p][:, c * E + f * 512:
                                                      c * E + (f + 1) * 512],
                                               start=(c == 0), stop=(c == HC_HALF - 1)))
                                    if hc == HC_HALF - 1:
                                        co.op("tensor", bld, inc=(sT, 1))
                                    else:
                                        co.raw("tensor", bld)
                                t_o = sT.n
                                pO_rd[ob] = co.op(
                                    "vector", lambda E_, b_=ob, f=foh, g=gt, ww=w,
                                    ee=e, oc_=ocy: E_.tensor_scalar_mul(
                                        Ost[oc_][:, f * 512:(f + 1) * 512], pO[b_][:],
                                        maskW[:, (g * 4 + ww) * NE + ee:
                                              (g * 4 + ww) * NE + ee + 1]),
                                    inc=(sV, 1),
                                    waits=[(sT, t_o), (d_yacc[ocy], ost_dma[ocy])])
                            v_o = sV.n
                            co.op("gpsimd", lambda E_, g=gt, ww=w, oc_=ocy:
                                  E_.dma_start(out=Ys[g * 4 + ww], in_=Ost[oc_][:],
                                               accum_op=ALU.add),
                                  inc=(d_yacc[ocy], 16), dma=True,
                                  waits=[(sV, v_o), d_ysi])
                            ost_dma[ocy] = d_yacc[ocy].n
                            ost_cyc += 1
                    nxt = e * 2 + half + 2
                    if nxt < NE * 2:
                        load_whalf(nxt // 2, nxt % 2)
            co.barrier([sT, sV, sS] + d_yacc)
            co.flush(blk)

        with ExitStack() as pe:
            Yld = [pe.enter_context(nc.sbuf_tensor(f"Yld{i}", [128, E], F32))
                   for i in range(2)]
            sqe = pe.enter_context(nc.sbuf_tensor("sqe", [128, E], F32))
            out_s = [pe.enter_context(nc.sbuf_tensor(f"out_s{i}", [128, E], F32))
                     for i in range(2)]
            stat2 = pe.enter_context(nc.sbuf_tensor("stat2", [128, 8], F32))

            y_ld = [0, 0]
            y_rd = [0, 0]
            o_dma = [0, 0]
            yacc_fulls = [(dy, dy.n) for dy in d_yacc]
            for r in range(NTOK // 128):
                par = r % 2
                co.op("scalar", lambda E_, rr=r, p=par: E_.dma_start(
                    out=Yld[p][:], in_=Ys[rr]),
                    inc=(d_ye[par], 16), dma=True,
                    waits=yacc_fulls + [(sV, y_rd[par])])
                y_ld[par] = d_ye[par].n
                co.op("vector", lambda E_, p=par: E_.reduce_sum(
                    stat2[:, 0:1], Yld[p][:], axis=AX.X),
                    inc=(sV, 1), waits=[(d_ye[par], y_ld[par])])
                co.op("vector", lambda E_: E_.tensor_scalar_mul(
                    stat2[:, 1:2], stat2[:, 0:1], 1.0 / E), inc=(sV, 1))
                ssq2 = co.op("scalar", lambda E_, p=par: E_.activation(
                    sqe[:], Yld[p][:], AF.Square, accum_out=stat2[:, 2:3]),
                    inc=(sS, 1), waits=[(d_ye[par], y_ld[par])])
                co.wait("vector", sS, ssq2)
                co.op("vector", lambda E_: E_.tensor_tensor(
                    stat2[:, 3:4], stat2[:, 1:2], stat2[:, 1:2], ALU.mult),
                    inc=(sV, 1))
                co.op("vector", lambda E_: E_.scalar_tensor_tensor(
                    stat2[:, 4:5], stat2[:, 2:3], 1.0 / E, stat2[:, 3:4],
                    ALU.mult, ALU.subtract), inc=(sV, 1))
                vv = co.op("vector", lambda E_: E_.tensor_scalar_add(
                    stat2[:, 5:6], stat2[:, 4:5], 1e-5), inc=(sV, 1))
                co.op("scalar", lambda E_: E_.activation(
                    stat2[:, 6:7], stat2[:, 5:6], AF.Ln), inc=(sS, 1),
                    waits=[(sV, vv)])
                ss2 = co.op("scalar", lambda E_: E_.activation(
                    stat2[:, 7:8], stat2[:, 6:7], AF.Exp, scale=-0.5),
                    inc=(sS, 1))
                co.wait("vector", sS, ss2)
                vout = co.op("vector", lambda E_, p=par: E_.tensor_scalar(
                    out_s[p][:], Yld[p][:], stat2[:, 1:2], stat2[:, 7:8],
                    ALU.subtract, ALU.mult), inc=(sV, 1),
                    waits=[(d_out[par], o_dma[par])])
                y_rd[par] = vout
                co.op("sync", lambda E_, rr=r, p=par: E_.dma_start(
                    out=y[rr * 128:(rr + 1) * 128, :], in_=out_s[p][:]),
                    inc=(d_out[par], 16), dma=True, waits=[(sV, vout)])
                o_dma[par] = d_out[par].n
            co.raw("sync", lambda E_, v=d_out[0].n: E_.wait_ge(d_out[0].h, v))
            co.raw("sync", lambda E_, v=d_out[1].n: E_.wait_ge(d_out[1].h, v))
            co.flush(blk)

    return nc


def kernel(**inputs):
    per_core, sh = host_prep(inputs)
    nc = build_nc()
    in_maps = []
    for c in range(NCORES):
        m = dict(sh)
        m.update(per_core[c])
        in_maps.append(m)
    res = run_bass_kernel_spmd(nc, in_maps, core_ids=list(range(NCORES)))
    out = np.concatenate(
        [np.asarray(res.results[c]["y"], dtype=np.float32).reshape(
            SEQ_PER_CORE, T, E) for c in range(NCORES)], axis=0)
    return out
